# version 01
# baseline (speedup 1.0000x reference)
import numpy as np
import ml_dtypes
from contextlib import ExitStack

import concourse.bass as bass
import concourse.mybir as mybir
from concourse.bass_utils import run_bass_kernel_spmd

F32 = mybir.dt.float32
BF16 = mybir.dt.bfloat16
AF = mybir.ActivationFunctionType
ALU = mybir.AluOpType
AX = mybir.AxisListType

B, S, D = 32, 2048, 1024
NCORE = 8
BPC = B // NCORE
W = 1024
NPASS = BPC * (S // W)
NTT = W // 128
DC = D // 128
H, DH = 16, 64
FF = 4096
NE, HID = 16, 512
SH = 2048
EPS = 1e-5

ENGS = ("sync", "tensor", "vector", "scalar", "gpsimd")


class Prog:
    def __init__(self, nc, stack):
        self.nc = nc
        self.q = {e: [] for e in ENGS}
        self.sems = {}
        self.cnt = {}
        self.lastw = {e: {} for e in ENGS}
        for s in ("pe", "dve", "act", "wl", "xl", "so"):
            self.sems[s] = stack.enter_context(nc.semaphore(s))
            self.cnt[s] = 0

    def do(self, eng, fn, waits=(), inc=None, amt=1):
        ws = []
        for t in waits:
            if t is None:
                continue
            s, v = t
            assert v <= self.cnt[s], f"wait on future {s}:{v}>{self.cnt[s]}"
            if self.lastw[eng].get(s, 0) < v:
                self.lastw[eng][s] = v
                ws.append((s, v))
        tok = None
        if inc is not None:
            self.cnt[inc] += amt
            tok = (inc, self.cnt[inc])
        self.q[eng].append((ws, fn, tok, amt))
        return tok

    def finalize(self, stack):
        block = stack.enter_context(self.nc.Block())
        for eng in ENGS:
            ops = self.q[eng]
            if not ops:
                continue

            def body(e, ops=ops):
                for ws, fn, tok, amt in ops:
                    for (s, v) in ws:
                        e.wait_ge(self.sems[s], v)
                    ins = fn(e)
                    if tok is not None:
                        ins.then_inc(self.sems[tok[0]], amt)
            getattr(block, eng)(body)


class Bank:
    def __init__(self, t):
        self.t = t
        self.wtok = None
        self.rtok = []


class Slot:
    def __init__(self, buf):
        self.buf = buf
        self.wtok = None
        self.rtok = None


def build(nc, npass=NPASS, maxphase=6, dbg=(), sub=9, p2cut=9):
    dt = nc.dram_tensor
    x_h = dt("x", [BPC, S, D], F32, kind="ExternalInput").ap()
    ins = {}
    for nm, shp in (("wq", [D, D]), ("wk", [D, D]), ("wv", [D, D]), ("wo", [D, D]),
                    ("proj", [128, 128]), ("w1", [D, FF]), ("w2", [FF, D]),
                    ("b1c", [128, FF // 128]), ("b2c", [128, DC]),
                    ("g1c", [128, DC]), ("be1c", [128, DC]),
                    ("g2c", [128, DC]), ("be2c", [128, DC]),
                    ("WgT", [D, NE]), ("WeGT", [NE, D, HID]), ("WeUT", [NE, D, HID]),
                    ("WeDT", [NE, HID, D]), ("WsGT", [D, SH]), ("WsUT", [D, SH]),
                    ("WsDT", [SH, D]), ("WsgT", [128, DC]),
                    ("idf", [128, 128]), ("maskT", [128, 128]),
                    ("onesf", [128, 128])):
        ins[nm] = dt(nm, shp, F32, kind="ExternalInput").ap()
    ins["onehot"] = dt("onehot", [NE, NE * 128], BF16, kind="ExternalInput").ap()
    onesb_h = dt("onesb", [128, 128], BF16, kind="ExternalInput").ap()
    y_h = dt("y", [BPC, S, D], F32, kind="ExternalOutput").ap()

    stack = ExitStack()
    with stack:
        sb = lambda n, shp, d=BF16: stack.enter_context(nc.sbuf_tensor(n, shp, d))
        idf = sb("s_idf", [128, 128], F32)
        maskT = sb("s_maskT", [128, 128], F32)
        onehot = sb("s_onehot", [NE, NE * 128], BF16)
        onesb = sb("s_onesb", [128, 128], BF16)
        onesf = sb("s_onesf", [128, 128], F32)
        proj = sb("projb", [128, 128], BF16)
        WgTs = sb("WgTs", [128, DC * NE], BF16)
        WsgTs = sb("WsgTs", [128, DC], BF16)
        b1c = sb("s_b1c", [128, FF // 128], F32)
        b2c = sb("s_b2c", [128, DC], F32)
        g1c = sb("s_g1c", [128, DC], F32)
        be1c = sb("s_be1c", [128, DC], F32)
        g2c = sb("s_g2c", [128, DC], F32)
        be2c = sb("s_be2c", [128, DC], F32)
        kvf = sb("kvf", [64, H * DH], F32)
        ksf = sb("ksf", [64, H], F32)
        kvb = sb("kvb", [128, H * DH], BF16)
        ksb = sb("ksb", [128, H], BF16)
        xn = sb("xn", [128, DC * W])
        hbuf = sb("hbuf", [128, DC * W])
        abuf = sb("abuf", [128, DC * W])
        ybuf = sb("ybuf", [128, DC * W])
        big4 = sb("big4", [128, 4 * DC * 512])
        qraw = sb("qraw", [128, DC * 512])
        sigB = sb("sigB", [128, W], BF16)
        wT = sb("wT", [NE, W], BF16)
        w_tm = sb("w_tm", [128, NTT * NE], F32)
        smallb = sb("smallb", [128, 1792], BF16)
        stats = sb("stats", [128, 64], F32)
        lnrow = sb("lnrow", [1, 1536], F32)
        lnrowb = sb("lnrowb", [1, 1024], BF16)
        drt = sb("drt", [1, 256], F32)
        rbt = sb("rbt", [64, 256], F32)
        scr = sb("scr", [128, 6208], F32)
        wslot = [sb(f"ws{i}", [128, 4096]) for i in range(6)]

        ps = [stack.enter_context(nc.psum_tensor(f"p{i}", [128, 512], F32))
              for i in range(8)]
        P = Prog(nc, stack)
        banks = [Bank(t) for t in ps]
        slots = [Slot(t) for t in wslot]

        xn_v = xn[:, :].rearrange("q (c w) -> q c w", w=W)
        h_v = hbuf[:, :].rearrange("q (c w) -> q c w", w=W)
        a_v = abuf[:, :].rearrange("q (c w) -> q c w", w=W)
        y_v = ybuf[:, :].rearrange("q (c w) -> q c w", w=W)

        def syncdma(dst, src, waits=()):
            return P.do("sync", lambda e, d=dst, s=src: e.dma_start(d, s),
                        waits=waits, inc="xl", amt=16)

        def storedma(dst, src, waits=()):
            return P.do("sync", lambda e, d=dst, s=src: e.dma_start(d, s),
                        waits=waits, inc="so", amt=16)

        def gpdma(dst, src, waits=()):
            return P.do("gpsimd", lambda e, d=dst, s=src: e.dma_start(d, s),
                        waits=waits, inc="wl", amt=16)

        def load_slot(sl, src):
            sl.wtok = gpdma(sl.buf[:, :], src, waits=[sl.rtok])
            sl.rtok = None
            return sl.wtok

        rot = {"b": 0}

        def nextbank():
            i = rot["b"]
            rot["b"] = (i + 1) % 6
            return banks[i]

        def mm_group(bank, mms):
            n = len(mms)
            tok = None
            for i, item in enumerate(mms):
                (out, l, r, ws), tp = (item[:4], item[4] if len(item) > 4 else None)
                w = list(ws)
                if i == 0:
                    w.extend(bank.rtok)

                def f(e, out=out, l=l, r=r, i=i, n=n, tp=tp):
                    return e.matmul(out, l, r, start=(i == 0), stop=(i == n - 1),
                                    tile_position=tp)
                tok = P.do("tensor", f, waits=w,
                           inc=("pe" if i == n - 1 else None))
            bank.wtok = tok
            return tok

        def tr_group(bank, items):
            tok = None
            for i, (out, inp, ws) in enumerate(items):
                w = list(ws)
                if i == 0:
                    w.extend(bank.rtok)

                def f(e, out=out, inp=inp):
                    return e.transpose(out, inp, idf[:, :])
                tok = P.do("tensor", f, waits=w,
                           inc=("pe" if i == len(items) - 1 else None))
            bank.wtok = tok
            return tok

        def consume(bank, eng, fn, waits=(), sem=None):
            sem = sem or ("dve" if eng == "vector" else "act")
            tok = P.do(eng, fn, waits=[bank.wtok] + list(waits), inc=sem)
            bank.rtok = [tok]
            return tok

        def penow():
            return ("pe", P.cnt["pe"])

        ctoks = [
            syncdma(idf[:, :], ins["idf"]),
            syncdma(maskT[:, :], ins["maskT"]),
            syncdma(onehot[:, :], ins["onehot"]),
            syncdma(onesb[:, :], onesb_h),
            syncdma(onesf[:, :], ins["onesf"]),
            syncdma(b1c[:, :], ins["b1c"]),
            syncdma(b2c[:, :], ins["b2c"]),
            syncdma(g1c[:, :], ins["g1c"]),
            syncdma(be1c[:, :], ins["be1c"]),
            syncdma(g2c[:, :], ins["g2c"]),
            syncdma(be2c[:, :], ins["be2c"]),
            gpdma(proj[:, :], ins["proj"]),
            gpdma(WgTs[:, :], ins["WgT"].rearrange("(c p) m -> p c m", p=128)),
            gpdma(WsgTs[:, :], ins["WsgT"]),
        ]
        ck = ctoks[-1]
        cs = ctoks[-4]
        t_eps = P.do("vector", lambda e: e.memset(stats[:, 63:64], EPS), inc="dve")

        wq_src = ins["wq"].rearrange("(c p) m -> p c m", p=128)
        wk_src = ins["wk"].rearrange("(c p) m -> p c m", p=128)
        wv_src = ins["wv"].rearrange("(c p) m -> p c m", p=128)
        wo_src = ins["wo"].rearrange("(c p) m -> p c m", p=128)
        w1_src = ins["w1"].rearrange("(c p) m -> p c m", p=128)
        w2_src = ins["w2"].rearrange("(c p) m -> p c m", p=128)
        WeG_src = [ins["WeGT"][e].rearrange("(c p) m -> p c m", p=128) for e in range(NE)]
        WeU_src = [ins["WeUT"][e].rearrange("(c p) m -> p c m", p=128) for e in range(NE)]
        WeD_src = [ins["WeDT"][e].rearrange("(c p) m -> p c m", p=128) for e in range(NE)]
        WsG_src = ins["WsGT"].rearrange("(c p) m -> p c m", p=128)
        WsU_src = ins["WsUT"].rearrange("(c p) m -> p c m", p=128)
        WsD_src = ins["WsDT"].rearrange("(c p) m -> p c m", p=128)

        dbgset = set(dbg)
        dbg_regs = {}

        def reg_dbg(name, ap, pdim, fdim, dtype):
            if name in dbgset and name not in dbg_regs:
                o = dt("dbg_" + name, [pdim, fdim], dtype, kind="ExternalOutput").ap()
                dbg_regs[name] = (ap, o)

        last_x_use = [None, None]
        outst_free = [None, None]
        state_tok = {h: None for h in range(H)}
        bar = {"pe": None, "dve": None, "act": None}

        for p in range(npass):
            seq, half = p // 2, p % 2
            row0 = half * W
            pbar = [bar["pe"], bar["dve"], bar["act"]]

            if half == 0:
                P.do("vector", lambda e: e.memset(kvf[:, :], 0.0), waits=pbar, inc="dve")
                P.do("vector", lambda e: e.memset(ksf[:, :], 0.0), inc="dve")
                P.do("vector", lambda e: e.memset(kvb[:, :], 0.0), inc="dve")
                z4 = P.do("vector", lambda e: e.memset(ksb[:, :], 0.0), inc="dve")
                for h in range(H):
                    state_tok[h] = z4
            xin = [scr[:, 2048:3072], scr[:, 3072:4096]]
            xload = [None] * NTT

            xt_tok = [None] * NTT
            ht_tok = [None] * NTT
            prev_trA = None
            prev_trB = None
            for tt in range(NTT):
                xload[tt] = syncdma(xin[tt % 2],
                                    x_h[seq, row0 + tt * 128: row0 + (tt + 1) * 128, :],
                                    waits=[last_x_use[tt % 2]])
                xt = xin[tt % 2]
                P.do("vector", lambda e, xt=xt: e.bn_stats(
                    stats[:, 16:22], xt[:, 0:512]),
                    waits=[xload[tt]] + pbar, inc="dve")
                P.do("vector", lambda e, xt=xt: e.bn_stats(
                    stats[:, 22:28], xt[:, 512:1024]), inc="dve")
                t4b = P.do("scalar", lambda e: e.copy(
                    stats[:, 40:52], stats[:, 16:28]),
                    waits=[("dve", P.cnt["dve"])], inc="act")
                t5 = P.do("vector", lambda e: e.bn_aggr(
                    stats[:, 0:2], stats[:, 40:52]), waits=[t4b], inc="dve")
                t5b = P.do("scalar", lambda e: e.copy(
                    stats[:, 2:4], stats[:, 0:2]), waits=[t5], inc="act")
                t6 = P.do("scalar", lambda e: e.activation(
                    stats[:, 5:6], stats[:, 3:4], AF.Ln, bias=stats[:, 63:64]),
                    waits=[t5b, t_eps], inc="act")
                t7 = P.do("scalar", lambda e: e.activation(
                    stats[:, 6:7], stats[:, 5:6], AF.Exp, scale=-0.5),
                    waits=[t6], inc="act")
                t8 = P.do("vector", lambda e, xt=xt: e.tensor_scalar(
                    scr[:, 0:1024], xt, stats[:, 2:3], stats[:, 6:7],
                    ALU.subtract, ALU.mult), waits=[t7, prev_trA], inc="dve")
                P.do("vector", lambda e: e.bn_stats(
                    stats[:, 16:22], scr[:, 0:512]), waits=[t8], inc="dve")
                P.do("vector", lambda e: e.bn_stats(
                    stats[:, 22:28], scr[:, 512:1024]), inc="dve")
                u4b = P.do("scalar", lambda e: e.copy(
                    stats[:, 40:52], stats[:, 16:28]),
                    waits=[("dve", P.cnt["dve"])], inc="act")
                u5 = P.do("vector", lambda e: e.bn_aggr(
                    stats[:, 8:10], stats[:, 40:52]), waits=[u4b], inc="dve")
                u5b = P.do("scalar", lambda e: e.copy(
                    stats[:, 10:12], stats[:, 8:10]), waits=[u5], inc="act")
                u6 = P.do("scalar", lambda e: e.activation(
                    stats[:, 12:13], stats[:, 11:12], AF.Ln, bias=stats[:, 63:64]),
                    waits=[u5b], inc="act")
                u7 = P.do("scalar", lambda e: e.activation(
                    stats[:, 13:14], stats[:, 12:13], AF.Exp, scale=-0.5),
                    waits=[u6], inc="act")
                u8 = P.do("vector", lambda e: e.tensor_scalar(
                    scr[:, 1024:2048], scr[:, 0:1024], stats[:, 10:11], stats[:, 13:14],
                    ALU.subtract, ALU.mult), waits=[u7, prev_trB], inc="dve")
                last_x_use[tt % 2] = t8
                for g in range(2):
                    bA = nextbank()
                    tr_group(bA, [(bA.t[:, j * 128:(j + 1) * 128],
                                   scr[:, (g * 4 + j) * 128:(g * 4 + j + 1) * 128],
                                   [t8, cs]) for j in range(4)])
                    prev_trA = bA.wtok
                    tokx = consume(bA, "scalar", lambda e, bA=bA, g=g, tt=tt: e.copy(
                        xn_v[:, g * 4:(g + 1) * 4, tt * 128:(tt + 1) * 128],
                        bA.t[:, 0:512].rearrange("q (c w) -> q c w", w=128)))
                    bB = nextbank()
                    tr_group(bB, [(bB.t[:, j * 128:(j + 1) * 128],
                                   scr[:, 1024 + (g * 4 + j) * 128:1024 + (g * 4 + j + 1) * 128],
                                   [u8]) for j in range(4)])
                    prev_trB = bB.wtok
                    tokh = None
                    for j in range(4):
                        d = g * 4 + j
                        tokh = P.do("vector", lambda e, bB=bB, j=j, d=d, tt=tt: e.tensor_scalar(
                            hbuf[:, d * W + tt * 128: d * W + (tt + 1) * 128],
                            bB.t[:, j * 128:(j + 1) * 128],
                            g1c[:, d:d + 1], be1c[:, d:d + 1], ALU.mult, ALU.add),
                            waits=[bB.wtok, cs], inc="dve")
                    bB.rtok = [tokh]
                    if g == 1:
                        xt_tok[tt] = tokx
                        ht_tok[tt] = tokh

            for tt in range(NTT):
                bG = nextbank()
                mm_group(bG, [(bG.t[:, 0:NE],
                               xn[:, d * W + tt * 128: d * W + (tt + 1) * 128],
                               WgTs[:, d * NE:(d + 1) * NE],
                               [xt_tok[tt], ck]) for d in range(DC)])
                gsc = scr[:, 6144:6160]
                m1s = scr[:, 6160:6176]
                g2s = scr[:, 6176:6192]
                m2s = scr[:, 6192:6208]
                a1 = consume(bG, "vector", lambda e, bG=bG: e.tensor_reduce(
                    stats[:, 32:33], bG.t[:, 0:NE], AX.X, ALU.max))
                a1b = P.do("scalar", lambda e: e.copy(stats[:, 40:41], stats[:, 32:33]),
                           waits=[a1], inc="act")
                a2 = P.do("vector", lambda e, bG=bG, gsc=gsc: e.tensor_scalar(
                    gsc, bG.t[:, 0:NE], stats[:, 40:41], None, ALU.subtract),
                    waits=[a1b], inc="dve")
                bG.rtok = [a2]
                a3 = P.do("scalar", lambda e, gsc=gsc: e.activation(gsc, gsc, AF.Exp),
                          waits=[a2], inc="act")
                a4 = P.do("vector", lambda e, gsc=gsc: e.tensor_reduce(
                    stats[:, 33:34], gsc, AX.X, ALU.add), waits=[a3], inc="dve")
                a4b = P.do("scalar", lambda e: e.copy(stats[:, 41:42], stats[:, 33:34]),
                           waits=[a4], inc="act")
                a5 = P.do("vector", lambda e: e.reciprocal(stats[:, 34:35], stats[:, 41:42]),
                          waits=[a4b], inc="dve")
                a5b = P.do("scalar", lambda e: e.copy(stats[:, 42:43], stats[:, 34:35]),
                           waits=[a5], inc="act")
                a6 = P.do("vector", lambda e, gsc=gsc: e.tensor_scalar(
                    gsc, gsc, stats[:, 42:43], None, ALU.mult), waits=[a5b], inc="dve")
                a7 = P.do("vector", lambda e, gsc=gsc: e.tensor_reduce(
                    stats[:, 35:36], gsc, AX.X, ALU.max), waits=[a6], inc="dve")
                a7b = P.do("scalar", lambda e: e.copy(stats[:, 43:44], stats[:, 35:36]),
                           waits=[a7], inc="act")
                a8 = P.do("vector", lambda e, gsc=gsc, m1s=m1s: e.tensor_scalar(
                    m1s, gsc, stats[:, 43:44], None, ALU.is_ge), waits=[a7b], inc="dve")
                a9 = P.do("vector", lambda e, gsc=gsc, m1s=m1s, g2s=g2s: e.scalar_tensor_tensor(
                    g2s, m1s, -2.0, gsc, ALU.mult, ALU.add), waits=[a8], inc="dve")
                aA = P.do("vector", lambda e, g2s=g2s: e.tensor_reduce(
                    stats[:, 36:37], g2s, AX.X, ALU.max), waits=[a9], inc="dve")
                aAb = P.do("scalar", lambda e: e.copy(stats[:, 44:45], stats[:, 36:37]),
                           waits=[aA], inc="act")
                aB = P.do("vector", lambda e, g2s=g2s, m2s=m2s: e.tensor_scalar(
                    m2s, g2s, stats[:, 44:45], None, ALU.is_ge), waits=[aAb], inc="dve")
                aC = P.do("vector", lambda e: e.tensor_tensor(
                    stats[:, 37:38], stats[:, 43:44], stats[:, 44:45], ALU.add),
                    waits=[aB], inc="dve")
                aD = P.do("vector", lambda e: e.reciprocal(stats[:, 38:39], stats[:, 37:38]),
                          waits=[aC], inc="dve")
                aDb = P.do("scalar", lambda e: e.copy(stats[:, 45:46], stats[:, 38:39]),
                           waits=[aD], inc="act")
                aE = P.do("vector", lambda e, m1s=m1s, m2s=m2s: e.tensor_tensor(
                    m1s, m1s, m2s, ALU.add), waits=[aB], inc="dve")
                aF = P.do("vector", lambda e, gsc=gsc, m1s=m1s, tt=tt: e.scalar_tensor_tensor(
                    w_tm[:, tt * NE:(tt + 1) * NE], gsc, stats[:, 45:46], m1s,
                    ALU.mult, ALU.mult), waits=[aDb, aE], inc="dve")
                bT = nextbank()
                tr_group(bT, [(bT.t[0:NE, 0:128], w_tm[:, tt * NE:(tt + 1) * NE], [aF, cs])])
                consume(bT, "scalar", lambda e, bT=bT, tt=tt: e.copy(
                    wT[0:NE, tt * 128:(tt + 1) * 128], bT.t[0:NE, 0:128]))
            wT_tok = ("act", P.cnt["act"])

            for th in range(2):
                bS = nextbank()
                mm_group(bS, [(bS.t[0:1, 0:512], WsgTs[:, d:d + 1],
                               xn_v[:, d, th * 512:(th + 1) * 512],
                               [xt_tok[th * 4 + 3], ck]) for d in range(DC)])
                sv = consume(bS, "scalar", lambda e, bS=bS: e.activation(
                    smallb[0:1, 1280:1792], bS.t[0:1, 0:512], AF.Sigmoid))
                bS2 = nextbank()
                mm_group(bS2, [(bS2.t[:, 0:512], onesb[0:1, :],
                                smallb[0:1, 1280:1792], [sv])])
                consume(bS2, "vector", lambda e, bS2=bS2, th=th: e.tensor_copy(
                    sigB[:, th * 512:(th + 1) * 512], bS2.t[:, 0:512]))
            sig_done = ("dve", P.cnt["dve"])

            reg_dbg("xn", xn[:, :], 128, DC * W, BF16)
            reg_dbg("h", hbuf[:, :], 128, DC * W, BF16)
            reg_dbg("w_tm", w_tm[:, :], 128, NTT * NE, F32)
            reg_dbg("sigB", sigB[:, :], 128, W, F32)
            if maxphase < 2:
                bar = {"pe": penow(), "dve": ("dve", P.cnt["dve"]), "act": ("act", P.cnt["act"])}
                continue

            q_of, k_of, v_of, kt_of = 0, DC * 512, 2 * DC * 512, 3 * DC * 512
            atw = {}
            for i, (nm, src) in enumerate((("wq", wq_src), ("wk", wk_src), ("wv", wv_src))):
                for hf in range(2):
                    sl = slots[i * 2 + hf]
                    load_slot(sl, src[:, :, hf * 512:(hf + 1) * 512])
                    atw[(nm, hf)] = sl

            qkv_free = None
            smr = [None, None]
            drr = [None, None]
            rbr = [None, None]
            attn_last = [None, None]
            scan_i = 0

            for st in range(2):
                phi_q_done = None
                phi_last = {}
                rev_k = None
                prev_elu = None
                projs = () if p2cut == 0 else ((("wq", q_of),) if p2cut in (1, 11, 12, 13, 14) else (("wq", q_of), ("wk", k_of)))
                for nm, dstof in projs:
                    rev_toks = []
                    for c in range(DC):
                        bR = nextbank()
                        mm_group(bR, [(bR.t[:, 0:512],
                                       atw[(nm, c // 4)].buf[:, d * 512 + (c % 4) * 128: d * 512 + (c % 4) * 128 + 128],
                                       h_v[:, d, st * 512:(st + 1) * 512],
                                       [ht_tok[st * 4 + 3], atw[(nm, c // 4)].wtok])
                                      for d in range(DC)])
                        rev = consume(bR, "vector", lambda e, bR=bR, c=c: e.tensor_copy(
                            qraw[:, c * 512:(c + 1) * 512], bR.t[:, 0:512]),
                            waits=[qkv_free, phi_q_done])
                        rev_toks.append(rev)
                        if p2cut == 11:
                            phi_last[nm] = rev
                            continue
                        bP = nextbank()
                        mm_group(bP, [(bP.t[:, 0:512], proj[:, :],
                                       qraw[:, c * 512:(c + 1) * 512], [rev, ck])])
                        if p2cut == 13:
                            ev0 = consume(bP, "vector", lambda e, bP=bP, dstof=dstof, c=c:
                                e.tensor_copy(big4[:, dstof + c * 512: dstof + (c + 1) * 512],
                                              bP.t[:, 0:512]), waits=[qkv_free])
                            prev_elu = ev0
                            phi_last[nm] = ev0
                            continue
                        e1 = P.do("vector", lambda e, bP=bP: e.tensor_scalar(
                            scr[:, 0:512], bP.t[:, 0:512], 0.0, None, ALU.max),
                            waits=[bP.wtok, prev_elu], inc="dve")
                        e2 = P.do("vector", lambda e, bP=bP: e.tensor_scalar(
                            scr[:, 1024:1536], bP.t[:, 0:512], 0.0, None, ALU.min),
                            waits=[bP.wtok], inc="dve")
                        bP.rtok = [e2]
                        e3 = P.do("scalar", lambda e: e.activation(
                            scr[:, 1024:1536], scr[:, 1024:1536], AF.Exp),
                            waits=[e2], inc="act")
                        e4 = P.do("vector", lambda e, dstof=dstof, c=c: e.tensor_tensor(
                            big4[:, dstof + c * 512: dstof + (c + 1) * 512],
                            scr[:, 0:512], scr[:, 1024:1536], ALU.add),
                            waits=[e1, e3, qkv_free], inc="dve")
                        prev_elu = e4
                        phi_last[nm] = e4
                    if nm == "wq":
                        phi_q_done = penow()
                    else:
                        rev_k = rev_toks
                ktm_toks = []
                for tc in range(4 if 3 <= p2cut <= 9 else 0):
                    for hg in range(2):
                        bK = nextbank()
                        mms = []
                        for pp in range(4):
                            pair = hg * 4 + pp
                            mms.append((bK.t[:, pp * 128:(pp + 1) * 128],
                                        qraw[:, pair * 512 + tc * 128: pair * 512 + (tc + 1) * 128],
                                        proj[:, :],
                                        [rev_k[pair], ck]))
                        mm_group(bK, mms)
                        f1 = P.do("vector", lambda e, bK=bK: e.tensor_scalar(
                            scr[:, 0:512], bK.t[:, 0:512], 0.0, None, ALU.max),
                            waits=[bK.wtok, prev_elu], inc="dve")
                        f2 = P.do("vector", lambda e, bK=bK: e.tensor_scalar(
                            scr[:, 1024:1536], bK.t[:, 0:512], 0.0, None, ALU.min),
                            waits=[bK.wtok], inc="dve")
                        bK.rtok = [f2]
                        f3 = P.do("scalar", lambda e: e.activation(
                            scr[:, 1024:1536], scr[:, 1024:1536], AF.Exp),
                            waits=[f2], inc="act")
                        f4 = P.do("vector", lambda e, tc=tc, hg=hg: e.tensor_tensor(
                            big4[:, kt_of + tc * 1024 + hg * 512: kt_of + tc * 1024 + (hg + 1) * 512],
                            scr[:, 0:512], scr[:, 1024:1536], ALU.add),
                            waits=[f1, f3, qkv_free], inc="dve")
                        prev_elu = f4
                        ktm_toks.append(f4)
                v_toks = {}
                for tc in range(4 if 4 <= p2cut <= 9 else 0):
                    for cf in range(2):
                        bV = nextbank()
                        mm_group(bV, [(bV.t[:, 0:512],
                                       h_v[:, d, st * 512 + tc * 128: st * 512 + (tc + 1) * 128],
                                       atw[("wv", cf)].buf[:, d * 512:(d + 1) * 512],
                                       [ht_tok[st * 4 + tc], atw[("wv", cf)].wtok])
                                      for d in range(DC)])
                        v_toks[(tc, cf)] = consume(bV, "scalar",
                            lambda e, bV=bV, tc=tc, cf=cf: e.copy(
                                big4[:, v_of + tc * 1024 + cf * 512: v_of + tc * 1024 + (cf + 1) * 512],
                                bV.t[:, 0:512]), waits=[qkv_free])
                if st == 1 and p2cut >= 1:
                    for key in atw:
                        atw[key].rtok = penow()

                if sub < 2:
                    qkv_free = penow()
                    attn_last[st] = ("dve", P.cnt["dve"])
                    continue
                for tc in range(4):
                    for h in range(H):
                        i2 = scan_i % 2
                        qf = big4[(h % 2) * 64:(h % 2) * 64 + 64,
                                  q_of + (h // 2) * 512 + tc * 128: q_of + (h // 2) * 512 + (tc + 1) * 128]
                        kf = big4[(h % 2) * 64:(h % 2) * 64 + 64,
                                  k_of + (h // 2) * 512 + tc * 128: k_of + (h // 2) * 512 + (tc + 1) * 128]
                        vt = big4[:, v_of + tc * 1024 + h * 64: v_of + tc * 1024 + (h + 1) * 64]
                        kt = big4[:, kt_of + tc * 1024 + h * 64: kt_of + tc * 1024 + (h + 1) * 64]
                        S_m = smallb[:, i2 * 128:(i2 + 1) * 128]
                        drs = drt[0:1, i2 * 128:(i2 + 1) * 128]
                        rbs = rbt[0:64, i2 * 128:(i2 + 1) * 128]
                        bS = nextbank()
                        mm_group(bS, [(bS.t[:, 0:128], kf, qf,
                                       [phi_last["wk"], phi_last["wq"]])])
                        smtok = consume(bS, "vector", lambda e, bS=bS, S_m=S_m: e.tensor_tensor(
                            S_m, bS.t[:, 0:128], maskT[:, :], ALU.mult),
                            waits=[smr[i2], cs])
                        bN = nextbank()
                        mm_group(bN, [(bN.t[0:64, 0:128], vt, S_m,
                                       [smtok, v_toks[(tc, h // 8)]]),
                                      (bN.t[0:64, 0:128],
                                       kvb[(h % 2) * 64:(h % 2) * 64 + 64, h * 64:(h + 1) * 64], qf,
                                       [state_tok[h]])])
                        numtok = bN.wtok
                        bD = nextbank()
                        mm_group(bD, [(bD.t[0:1, 0:128], onesb[:, 0:1], S_m, [smtok]),
                                      (bD.t[0:1, 0:128],
                                       ksb[(h % 2) * 64:(h % 2) * 64 + 64, h:h + 1], qf,
                                       [state_tok[h]])])
                        dentok = bD.wtok
                        smr[i2] = dentok
                        d1 = consume(bD, "vector", lambda e, bD=bD, drs=drs: e.tensor_scalar(
                            drs, bD.t[0:1, 0:128], 1e-6, None, ALU.add),
                            waits=[])
                        d2 = P.do("vector", lambda e, drs=drs: e.reciprocal(drs, drs),
                                  inc="dve")
                        drsb = lnrow[0:1, i2 * 128:(i2 + 1) * 128]
                        d3 = P.do("scalar", lambda e, drs=drs, drsb=drsb: e.copy(drsb, drs),
                                  waits=[d2, drr[i2]], inc="act")
                        bB = nextbank()
                        mm_group(bB, [(bB.t[0:64, 0:128], onesf[0:1, 0:64],
                                       drsb, [d3, cs])])
                        drr[i2] = bB.wtok
                        rbtok = consume(bB, "scalar", lambda e, bB=bB, rbs=rbs: e.copy(
                            rbs, bB.t[0:64, 0:128]), waits=[rbr[i2]])
                        atok = P.do("vector", lambda e, bN=bN, rbs=rbs, h=h, tc=tc, st=st:
                            e.tensor_tensor(
                                abuf[(h % 2) * 64:(h % 2) * 64 + 64,
                                     (h // 2) * W + st * 512 + tc * 128:(h // 2) * W + st * 512 + (tc + 1) * 128],
                                bN.t[0:64, 0:128], rbs, ALU.mult),
                            waits=[numtok, rbtok], inc="dve")
                        bN.rtok = [atok]
                        rbr[i2] = atok
                        attn_last[st] = atok
                        bU = nextbank()
                        mm_group(bU, [(bU.t[0:64, 0:64], kt, vt,
                                       [ktm_toks[tc * 2 + h // 8], v_toks[(tc, h // 8)]]),
                                      (bU.t[0:64, 64:65], kt, onesb[:, 0:1], [])])
                        kva = P.do("vector", lambda e, bU=bU, h=h: e.tensor_tensor(
                            kvf[:, h * 64:(h + 1) * 64], kvf[:, h * 64:(h + 1) * 64],
                            bU.t[0:64, 0:64], ALU.add), waits=[bU.wtok], inc="dve")
                        ksa = P.do("vector", lambda e, bU=bU, h=h: e.tensor_tensor(
                            ksf[:, h:h + 1], ksf[:, h:h + 1], bU.t[0:64, 64:65], ALU.add),
                            waits=[bU.wtok], inc="dve")
                        bU.rtok = [ksa]
                        hb_ = (h % 2) * 64
                        P.do("scalar", lambda e, h=h, hb_=hb_: e.copy(
                            kvb[hb_:hb_ + 64, h * 64:(h + 1) * 64],
                            kvf[:, h * 64:(h + 1) * 64]),
                            waits=[kva, dentok], inc="act")
                        c2 = P.do("scalar", lambda e, h=h, hb_=hb_: e.copy(
                            ksb[hb_:hb_ + 64, h:h + 1], ksf[:, h:h + 1]),
                            waits=[ksa, dentok], inc="act")
                        state_tok[h] = c2
                        scan_i += 1
                qkv_free = penow()

            reg_dbg("big4", big4[:, :], 128, 4 * DC * 512, BF16)
            if sub < 3:
                bar = {"pe": penow(), "dve": ("dve", P.cnt["dve"]), "act": ("act", P.cnt["act"])}
                continue
            for hf in range(2):
                sl = slots[hf]
                load_slot(sl, wo_src[:, :, hf * 512:(hf + 1) * 512])
                atw[("wo", hf)] = sl
            x1_done = None
            for st in range(2):
                for c in range(DC):
                    bW = nextbank()
                    mm_group(bW, [(bW.t[:, 0:512],
                                   atw[("wo", c // 4)].buf[:, d * 512 + (c % 4) * 128: d * 512 + (c % 4) * 128 + 128],
                                   a_v[:, d, st * 512:(st + 1) * 512],
                                   [attn_last[st], atw[("wo", c // 4)].wtok])
                                  for d in range(DC)])
                    x1_done = consume(bW, "vector", lambda e, bW=bW, c=c, st=st:
                        e.tensor_tensor(
                            y_v[:, c, st * 512:(st + 1) * 512],
                            bW.t[:, 0:512],
                            xn_v[:, c, st * 512:(st + 1) * 512], ALU.add))
            for hf in range(2):
                atw[("wo", hf)].rtok = penow()

            reg_dbg("attn", abuf[:, :], 128, DC * W, BF16)
            reg_dbg("x1", ybuf[:, :], 128, DC * W, BF16)
            if maxphase < 3:
                bar = {"pe": penow(), "dve": ("dve", P.cnt["dve"]), "act": ("act", P.cnt["act"])}
                continue

            h2_guard = penow()
            h2_toks = [None, None]
            for th in range(2):
                bL = nextbank()
                mm_group(bL, [(bL.t[0:1, 0:512], onesb[:, 0:1],
                               y_v[:, d, th * 512:(th + 1) * 512], [x1_done])
                              for d in range(DC)])
                bQ = nextbank()
                sq_prev_mm = None
                for dg in range(4):
                    sq_toks = []
                    for dl in range(2):
                        d = dg * 2 + dl
                        sqb = smallb[:, 256 + dl * 512:256 + (dl + 1) * 512]
                        sqt = P.do("scalar", lambda e, d=d, th=th, sqb=sqb: e.activation(
                            sqb, y_v[:, d, th * 512:(th + 1) * 512], AF.Square),
                            waits=[x1_done, sq_prev_mm], inc="act")
                        sq_toks.append((sqb, sqt))
                    for dl, (sqb, sqt) in enumerate(sq_toks):
                        ww = [sqt]
                        if dg == 0 and dl == 0:
                            ww.extend(bQ.rtok)

                        def f(e, sqb=sqb, dg=dg, dl=dl, bQ=bQ):
                            return e.matmul(bQ.t[0:1, 0:512], onesb[:, 0:1], sqb,
                                            start=(dg == 0 and dl == 0),
                                            stop=(dg == 3 and dl == 1))
                        tok = P.do("tensor", f, waits=ww,
                                   inc=("pe" if dl == 1 else None))
                    sq_prev_mm = tok
                bQ.wtok = tok
                m_ = lnrow[0:1, 0:512]
                msq = lnrow[0:1, 512:1024]
                var = lnrow[0:1, 1024:1536]
                rstd = lnrow[0:1, 1024:1536]
                l1 = consume(bL, "vector", lambda e, bL=bL, m_=m_: e.tensor_scalar(
                    m_, bL.t[0:1, 0:512], 1.0 / D, None, ALU.mult))
                l2 = P.do("vector", lambda e, m_=m_, msq=msq: e.tensor_tensor(
                    msq, m_, m_, ALU.mult), waits=[l1], inc="dve")
                l3a = consume(bQ, "vector", lambda e, bQ=bQ, var=var: e.tensor_scalar(
                    var, bQ.t[0:1, 0:512], 1.0 / D, None, ALU.mult),
                    waits=[l2])
                l3 = P.do("vector", lambda e, var=var, msq=msq: e.tensor_tensor(
                    var, var, msq, ALU.subtract), inc="dve")
                l4 = P.do("scalar", lambda e, var=var: e.activation(
                    var, var, AF.Ln, bias=stats[0:1, 63:64]), waits=[l3], inc="act")
                l5 = P.do("scalar", lambda e, var=var, rstd=rstd: e.activation(
                    rstd, var, AF.Exp, scale=-0.5), waits=[l4], inc="act")
                l6 = P.do("vector", lambda e, m_=m_, rstd=rstd, msq=msq: e.tensor_tensor(
                    msq, m_, rstd, ALU.mult), waits=[l5], inc="dve")
                l7 = P.do("scalar", lambda e, rstd=rstd: e.copy(
                    lnrowb[0:1, 0:512], rstd), waits=[l5], inc="act")
                l8 = P.do("scalar", lambda e, msq=msq: e.copy(
                    lnrowb[0:1, 512:1024], msq), waits=[l6], inc="act")
                bR1 = nextbank()
                mm_group(bR1, [(bR1.t[:, 0:512], onesb[0:1, :], lnrowb[0:1, 0:512], [l7])])
                bR2 = nextbank()
                mm_group(bR2, [(bR2.t[:, 0:512], onesb[0:1, :], lnrowb[0:1, 512:1024], [l8])])
                htk = None
                for d in range(DC):
                    w1_ = P.do("vector", lambda e, bR1=bR1, d=d, th=th: e.tensor_tensor(
                        scr[:, 0:512], y_v[:, d, th * 512:(th + 1) * 512],
                        bR1.t[:, 0:512], ALU.mult),
                        waits=[bR1.wtok, x1_done], inc="dve")
                    w2_ = P.do("vector", lambda e, bR2=bR2: e.tensor_tensor(
                        scr[:, 1024:1536], scr[:, 0:512], bR2.t[:, 0:512], ALU.subtract),
                        waits=[bR2.wtok, w1_], inc="dve")
                    htk = P.do("vector", lambda e, d=d, th=th: e.tensor_scalar(
                        hbuf[:, d * W + th * 512: d * W + (th + 1) * 512],
                        scr[:, 1024:1536],
                        g2c[:, d:d + 1], be2c[:, d:d + 1], ALU.mult, ALU.add),
                        waits=[w2_, h2_guard, cs], inc="dve")
                bR1.rtok = [htk]
                bR2.rtok = [htk]
                h2_toks[th] = htk

            reg_dbg("h2", hbuf[:, :], 128, DC * W, BF16)
            gelu_guard = penow()
            s_done = None
            for st in range(2):
                gel_toks = []
                w1sl = [slots[i] for i in range(3)]
                w2sl = [slots[3 + i] for i in range(3)]
                for fb in range(3):
                    load_slot(w1sl[fb], w1_src[:, :, fb * 512:(fb + 1) * 512])
                for fb in range(3):
                    load_slot(w2sl[fb], w2_src[:, fb * 4:(fb + 1) * 4, :])
                for fc in range(32):
                    fb = fc // 4
                    sl = w1sl[fb % 3]
                    bF = nextbank()
                    mm_group(bF, [(bF.t[:, 0:512],
                                   sl.buf[:, d * 512 + (fc % 4) * 128: d * 512 + (fc % 4) * 128 + 128],
                                   h_v[:, d, st * 512:(st + 1) * 512],
                                   [h2_toks[st], sl.wtok]) for d in range(DC)])
                    if fc % 4 == 3:
                        sl.rtok = bF.wtok
                        if fb + 3 <= 7:
                            load_slot(sl, w1_src[:, :, (fb + 3) * 512:(fb + 4) * 512])
                    gel_toks.append(consume(bF, "scalar", lambda e, bF=bF, fc=fc: e.activation(
                        big4[:, fc * 512:(fc + 1) * 512], bF.t[:, 0:512], AF.Gelu,
                        bias=b1c[:, fc:fc + 1]), waits=[gelu_guard, cs]))
                for fc in range(32):
                    fb = fc // 4
                    sl = w2sl[fb % 3]
                    for dout in range(DC):
                        bk = banks[dout]
                        w = [gel_toks[fc], sl.wtok]
                        if fc == 0:
                            w.extend(bk.rtok)

                        def f(e, bk=bk, sl=sl, fc=fc, dout=dout):
                            return e.matmul(
                                bk.t[:, 0:512],
                                sl.buf[:, (fc % 4) * 1024 + dout * 128:(fc % 4) * 1024 + (dout + 1) * 128],
                                big4[:, fc * 512:(fc + 1) * 512],
                                start=(fc == 0), stop=(fc == 31))
                        tok = P.do("tensor", f, waits=w,
                                   inc=("pe" if dout == DC - 1 else None))
                    if fc == 31:
                        for dout in range(DC):
                            banks[dout].wtok = tok
                    if fc % 4 == 3:
                        sl.rtok = tok
                        if fb + 3 <= 7:
                            load_slot(sl, w2_src[:, (fb + 3) * 4:(fb + 4) * 4, :])
                for dout in range(DC):
                    bk = banks[dout]
                    s_done = consume(bk, "vector", lambda e, bk=bk, dout=dout, st=st:
                        e.scalar_tensor_tensor(
                            a_v[:, dout, st * 512:(st + 1) * 512],
                            bk.t[:, 0:512], b2c[:, dout:dout + 1],
                            y_v[:, dout, st * 512:(st + 1) * 512],
                            ALU.add, ALU.add), waits=[cs])
                gelu_guard = penow()
            reg_dbg("s", abuf[:, :], 128, DC * W, BF16)
            if maxphase < 4:
                bar = {"pe": penow(), "dve": ("dve", P.cnt["dve"]), "act": ("act", P.cnt["act"])}
                continue

            x1_free = s_done
            hall_guard = penow()
            y_done = None
            for batch in range(4):
                es = [batch * 4 + i for i in range(4)]
                gsl, usl, dsl = {}, {}, {}
                for i, e_ in enumerate(es[:3]):
                    gsl[e_] = slots[2 * i]
                    load_slot(slots[2 * i], WeG_src[e_])
                    usl[e_] = slots[2 * i + 1]
                    load_slot(slots[2 * i + 1], WeU_src[e_])
                gsl[es[3]], usl[es[3]] = slots[0], slots[1]
                dsl[es[0]], dsl[es[1]] = slots[2], slots[3]
                dsl[es[2]], dsl[es[3]] = slots[4], slots[5]
                a_done = {}
                for ei, e_ in enumerate(es):
                    wb = {0: banks[6], 1: banks[7]}
                    mm_group(wb[0], [(wb[0].t[:, 0:512],
                                      onehot[0:NE, e_ * 128:(e_ + 1) * 128],
                                      wT[0:NE, 0:512], [wT_tok, cs])])
                    mm_group(wb[1], [(wb[1].t[:, 0:512],
                                      onehot[0:NE, e_ * 128:(e_ + 1) * 128],
                                      wT[0:NE, 512:1024], [wT_tok])])
                    last_h = None
                    prev_t2 = None
                    for th in range(2):
                        for hc in range(4):
                            bG = nextbank()
                            mm_group(bG, [(bG.t[:, 0:512],
                                           gsl[e_].buf[:, d * 512 + hc * 128: d * 512 + hc * 128 + 128],
                                           xn_v[:, d, th * 512:(th + 1) * 512],
                                           [gsl[e_].wtok, xt_tok[th * 4 + 3]]) for d in range(DC)])
                            bU = nextbank()
                            mm_group(bU, [(bU.t[:, 0:512],
                                           usl[e_].buf[:, d * 512 + hc * 128: d * 512 + hc * 128 + 128],
                                           xn_v[:, d, th * 512:(th + 1) * 512],
                                           [usl[e_].wtok]) for d in range(DC)])
                            sg = consume(bG, "scalar", lambda e, bG=bG: e.activation(
                                scr[:, 0:512], bG.t[:, 0:512], AF.Silu), waits=[prev_t2])
                            t2 = consume(bU, "vector", lambda e, bU=bU: e.tensor_tensor(
                                scr[:, 1024:1536], scr[:, 0:512], bU.t[:, 0:512], ALU.mult),
                                waits=[sg])
                            prev_t2 = t2
                            last_h = P.do("vector", lambda e, wbb=wb[th], ei=ei, hc=hc, th=th:
                                e.tensor_tensor(
                                    big4[:, ei * 4096 + hc * 1024 + th * 512:
                                         ei * 4096 + hc * 1024 + (th + 1) * 512],
                                    scr[:, 1024:1536], wbb.t[:, 0:512], ALU.mult),
                                waits=[wb[th].wtok, t2, hall_guard, x1_free], inc="dve")
                            a_done[(ei, th, hc)] = last_h
                    wb[0].rtok = [last_h]
                    wb[1].rtok = [last_h]
                    gsl[e_].rtok = penow()
                    usl[e_].rtok = penow()
                    if ei == 0:
                        load_slot(slots[0], WeG_src[es[3]])
                        load_slot(slots[1], WeU_src[es[3]])
                    elif ei == 1:
                        load_slot(slots[2], WeD_src[es[0]])
                        load_slot(slots[3], WeD_src[es[1]])
                    elif ei == 2:
                        load_slot(slots[4], WeD_src[es[2]])
                        load_slot(slots[5], WeD_src[es[3]])
                for th in range(2):
                    for dout in range(DC):
                        bk = banks[dout]
                        idx = 0
                        tok = None
                        for ei, e_ in enumerate(es):
                            for hc in range(4):
                                w = [dsl[e_].wtok, a_done[(ei, th, hc)]]
                                if idx == 0:
                                    w.extend(bk.rtok)

                                def f(e, bk=bk, sl=dsl[e_], ei=ei, hc=hc, th=th, dout=dout,
                                      idx=idx):
                                    return e.matmul(
                                        bk.t[:, 0:512],
                                        sl.buf[:, hc * 1024 + dout * 128: hc * 1024 + (dout + 1) * 128],
                                        big4[:, ei * 4096 + hc * 1024 + th * 512:
                                             ei * 4096 + hc * 1024 + (th + 1) * 512],
                                        start=(idx == 0), stop=(idx == 15))
                                tok = P.do("tensor", f, waits=w,
                                           inc=("pe" if idx == 15 else None))
                                idx += 1
                        bk.wtok = tok
                        if batch == 0:
                            y_done = consume(bk, "vector",
                                lambda e, bk=bk, dout=dout, th=th: e.tensor_copy(
                                    y_v[:, dout, th * 512:(th + 1) * 512], bk.t[:, 0:512]),
                                waits=[x1_free])
                        else:
                            y_done = consume(bk, "vector",
                                lambda e, bk=bk, dout=dout, th=th: e.tensor_tensor(
                                    y_v[:, dout, th * 512:(th + 1) * 512],
                                    y_v[:, dout, th * 512:(th + 1) * 512],
                                    bk.t[:, 0:512], ALU.add))
                for e_ in es:
                    dsl[e_].rtok = penow()
                hall_guard = penow()
            reg_dbg("ymoe", ybuf[:, :], 128, DC * W, BF16)
            if maxphase < 5:
                bar = {"pe": penow(), "dve": ("dve", P.cnt["dve"]), "act": ("act", P.cnt["act"])}
                continue

            hb_guard = penow()
            prev_hbm = None
            hb_dst = [hbuf, big4]
            sG, sU, sD = {}, {}, {}
            for j in range(3):
                sG[j] = slots[2 * j]
                load_slot(slots[2 * j], WsG_src[:, :, j * 512:(j + 1) * 512])
                sU[j] = slots[2 * j + 1]
                load_slot(slots[2 * j + 1], WsU_src[:, :, j * 512:(j + 1) * 512])
            sG[3], sU[3] = slots[0], slots[1]
            sD[0], sD[1], sD[2], sD[3] = slots[2], slots[3], slots[4], slots[5]
            hb_toks = {}
            for j in range(4):
                for hcl in range(4):
                    hc = j * 4 + hcl
                    for th in range(2):
                        bG = nextbank()
                        mm_group(bG, [(bG.t[:, 0:512],
                                       sG[j].buf[:, d * 512 + hcl * 128: d * 512 + hcl * 128 + 128],
                                       xn_v[:, d, th * 512:(th + 1) * 512],
                                       [sG[j].wtok, xt_tok[th * 4 + 3]]) for d in range(DC)])
                        bU = nextbank()
                        mm_group(bU, [(bU.t[:, 0:512],
                                       sU[j].buf[:, d * 512 + hcl * 128: d * 512 + hcl * 128 + 128],
                                       xn_v[:, d, th * 512:(th + 1) * 512],
                                       [sU[j].wtok]) for d in range(DC)])
                        sg = consume(bG, "scalar", lambda e, bG=bG: e.activation(
                            scr[:, 0:512], bG.t[:, 0:512], AF.Silu), waits=[prev_hbm])
                        hb_toks[(th, hc)] = consume(bU, "vector",
                            lambda e, bU=bU, hc=hc, th=th: e.tensor_tensor(
                                hb_dst[th][:, hc * 512:(hc + 1) * 512],
                                scr[:, 0:512], bU.t[:, 0:512], ALU.mult),
                            waits=[sg, hb_guard])
                        prev_hbm = hb_toks[(th, hc)]
                sG[j].rtok = penow()
                sU[j].rtok = penow()
                if j == 0:
                    load_slot(slots[0], WsG_src[:, :, 3 * 512:4 * 512])
                    load_slot(slots[1], WsU_src[:, :, 3 * 512:4 * 512])
                elif j == 1:
                    load_slot(slots[2], WsD_src[:, 0:4, :])
                    load_slot(slots[3], WsD_src[:, 4:8, :])
                elif j == 2:
                    load_slot(slots[4], WsD_src[:, 8:12, :])
                    load_slot(slots[5], WsD_src[:, 12:16, :])
            g_done = None
            for th in range(2):
                for dout in range(DC):
                    bk = banks[dout]
                    idx = 0
                    tok = None
                    for hc in range(16):
                        w = [sD[hc // 4].wtok, hb_toks[(th, hc)]]
                        if idx == 0:
                            w.extend(bk.rtok)

                        def f(e, bk=bk, sl=sD[hc // 4], hc=hc, th=th, dout=dout, idx=idx):
                            return e.matmul(
                                bk.t[:, 0:512],
                                sl.buf[:, (hc % 4) * 1024 + dout * 128:(hc % 4) * 1024 + (dout + 1) * 128],
                                hb_dst[th][:, hc * 512:(hc + 1) * 512],
                                start=(idx == 0), stop=(idx == 15))
                        tok = P.do("tensor", f, waits=w,
                                   inc=("pe" if idx == 15 else None))
                        idx += 1
                    bk.wtok = tok
                    v1 = consume(bk, "vector", lambda e, bk=bk, th=th: e.tensor_tensor(
                        scr[:, 0:512], bk.t[:, 0:512],
                        sigB[:, th * 512:(th + 1) * 512], ALU.mult),
                        waits=[sig_done])
                    v2 = P.do("vector", lambda e, dout=dout, th=th: e.tensor_tensor(
                        scr[:, 1024:1536], scr[:, 0:512],
                        y_v[:, dout, th * 512:(th + 1) * 512], ALU.add),
                        waits=[v1, y_done, g_done], inc="dve")
                    g_done = P.do("scalar", lambda e, dout=dout, th=th: e.activation(
                        y_v[:, dout, th * 512:(th + 1) * 512],
                        scr[:, 1024:1536], AF.Silu), waits=[v2], inc="act")
            for j in range(4):
                sD[j].rtok = penow()
            reg_dbg("g", ybuf[:, :], 128, DC * W, BF16)
            if maxphase < 6:
                bar = {"pe": penow(), "dve": ("dve", P.cnt["dve"]), "act": ("act", P.cnt["act"])}
                continue

            prev_ptr = None
            for tt in range(NTT):
                xr = syncdma(xin[tt % 2],
                             x_h[seq, row0 + tt * 128: row0 + (tt + 1) * 128, :],
                             waits=[last_x_use[tt % 2]])
                pv = P.do("vector", lambda e, tt=tt: e.tensor_tensor(
                    scr[:, 0:1024].rearrange("q (c w) -> q c w", w=128),
                    a_v[:, :, tt * 128:(tt + 1) * 128],
                    y_v[:, :, tt * 128:(tt + 1) * 128], ALU.mult),
                    waits=[g_done, s_done, prev_ptr], inc="dve")
                otoks = []
                for g in range(2):
                    bT = nextbank()
                    tr_group(bT, [(bT.t[:, j * 128:(j + 1) * 128],
                                   scr[:, (g * 4 + j) * 128:(g * 4 + j + 1) * 128],
                                   [pv, cs]) for j in range(4)])
                    prev_ptr = bT.wtok
                    otoks.append(consume(bT, "vector",
                        lambda e, bT=bT, tt=tt, g=g: e.tensor_tensor(
                            scr[:, 4096 + (tt % 2) * 1024 + g * 512:
                                4096 + (tt % 2) * 1024 + (g + 1) * 512],
                            bT.t[:, 0:512],
                            xin[tt % 2][:, g * 512:(g + 1) * 512], ALU.add),
                        waits=[xr, outst_free[tt % 2]]))
                stok = storedma(y_h[seq, row0 + tt * 128: row0 + (tt + 1) * 128, :],
                                scr[:, 4096 + (tt % 2) * 1024: 4096 + (tt % 2 + 1) * 1024],
                                waits=otoks)
                outst_free[tt % 2] = stok
                last_x_use[tt % 2] = otoks[-1]
            bar = {"pe": penow(), "dve": ("dve", P.cnt["dve"]), "act": ("act", P.cnt["act"])}

        waitall = [("pe", P.cnt["pe"]), ("dve", P.cnt["dve"]), ("act", P.cnt["act"]),
                   ("wl", P.cnt["wl"]), ("xl", P.cnt["xl"])]
        for name, (ap, out) in dbg_regs.items():
            P.do("gpsimd", lambda e, ap=ap, out=out: e.dma_start(out, ap),
                 waits=waitall, inc="so", amt=16)
        P.do("gpsimd", lambda e: e.nop(),
             waits=[("so", P.cnt["so"])] + waitall)
        P.finalize(stack)
    return nc


def _host_pack(inputs):
    f = np.float32
    asc = np.ascontiguousarray
    d = {}
    d["wq"] = asc(inputs["L0_attn_wq"], dtype=f)
    d["wk"] = asc(inputs["L0_attn_wk"], dtype=f)
    d["wv"] = asc(inputs["L0_attn_wv"], dtype=f)
    d["wo"] = asc(inputs["L0_attn_wo"], dtype=f)
    pj = np.zeros((128, 128), dtype=f)
    pj[0:64, 0:64] = np.asarray(inputs["L0_attn_proj"], dtype=f)
    pj[64:128, 64:128] = pj[0:64, 0:64]
    d["proj"] = pj
    d["w1"] = asc(inputs["L0_attn_w1"], dtype=f)
    d["w2"] = asc(inputs["L0_attn_w2"], dtype=f)
    d["b1c"] = asc(np.asarray(inputs["L0_attn_b1"], dtype=f).reshape(FF // 128, 128).T)
    d["b2c"] = asc(np.asarray(inputs["L0_attn_b2"], dtype=f).reshape(DC, 128).T)
    d["g1c"] = asc(np.asarray(inputs["L0_attn_g1"], dtype=f).reshape(DC, 128).T)
    d["be1c"] = asc(np.asarray(inputs["L0_attn_beta1"], dtype=f).reshape(DC, 128).T)
    d["g2c"] = asc(np.asarray(inputs["L0_attn_g2"], dtype=f).reshape(DC, 128).T)
    d["be2c"] = asc(np.asarray(inputs["L0_attn_beta2"], dtype=f).reshape(DC, 128).T)
    d["WgT"] = asc(np.asarray(inputs["L0_ffn_Wg"], dtype=f).T)
    d["WeGT"] = asc(np.asarray(inputs["L0_ffn_We_gate"], dtype=f).transpose(0, 2, 1))
    d["WeUT"] = asc(np.asarray(inputs["L0_ffn_We_up"], dtype=f).transpose(0, 2, 1))
    d["WeDT"] = asc(np.asarray(inputs["L0_ffn_We_down"], dtype=f).transpose(0, 2, 1))
    d["WsGT"] = asc(np.asarray(inputs["L0_ffn_Ws_gate"], dtype=f).T)
    d["WsUT"] = asc(np.asarray(inputs["L0_ffn_Ws_up"], dtype=f).T)
    d["WsDT"] = asc(np.asarray(inputs["L0_ffn_Ws_down"], dtype=f).T)
    d["WsgT"] = asc(np.asarray(inputs["L0_ffn_Wsg"], dtype=f).reshape(DC, 128).T)
    d["idf"] = np.eye(128, dtype=f)
    d["maskT"] = np.triu(np.ones((128, 128), dtype=f))
    d["onehot"] = asc(np.kron(np.eye(NE), np.ones((1, 128))).astype(ml_dtypes.bfloat16))
    d["onesf"] = np.ones((128, 128), dtype=f)
    d["onesb"] = np.ones((128, 128), dtype=ml_dtypes.bfloat16)
    return d


def kernel(**inputs):
    x = np.ascontiguousarray(np.asarray(inputs["L0_attn_x"], dtype=np.float32))
    shared = _host_pack(inputs)
    nc = bass.Bass("TRN2", target_bir_lowering=False)
    build(nc)
    in_maps = []
    for c in range(NCORE):
        m = dict(shared)
        m["x"] = np.ascontiguousarray(x[c * BPC:(c + 1) * BPC])
        in_maps.append(m)
    res = run_bass_kernel_spmd(nc, in_maps, core_ids=list(range(NCORE)))
    return np.concatenate([r["y"] for r in res.results], axis=0)


# version 02
# speedup vs baseline: 1.1204x; 1.1204x over previous
import numpy as np
import ml_dtypes
from contextlib import ExitStack

import concourse.bass as bass
import concourse.mybir as mybir
from concourse.bass_utils import run_bass_kernel_spmd

F32 = mybir.dt.float32
BF16 = mybir.dt.bfloat16
AF = mybir.ActivationFunctionType
ALU = mybir.AluOpType
AX = mybir.AxisListType

B, S, D = 32, 2048, 1024
NCORE = 8
BPC = B // NCORE
W = 1024
NPASS = BPC * (S // W)
NTT = W // 128
DC = D // 128
H, DH = 16, 64
FF = 4096
NE, HID = 16, 512
SH = 2048
EPS = 1e-5

ENGS = ("sync", "tensor", "vector", "scalar", "gpsimd")


class Prog:
    def __init__(self, nc, stack):
        self.nc = nc
        self.q = {e: [] for e in ENGS}
        self.sems = {}
        self.cnt = {}
        self.lastw = {e: {} for e in ENGS}
        for s in ("pe", "dve", "act", "wl", "xl", "so"):
            self.sems[s] = stack.enter_context(nc.semaphore(s))
            self.cnt[s] = 0

    def do(self, eng, fn, waits=(), inc=None, amt=1):
        ws = []
        for t in waits:
            if t is None:
                continue
            s, v = t
            assert v <= self.cnt[s], f"wait on future {s}:{v}>{self.cnt[s]}"
            if self.lastw[eng].get(s, 0) < v:
                self.lastw[eng][s] = v
                ws.append((s, v))
        tok = None
        if inc is not None:
            self.cnt[inc] += amt
            tok = (inc, self.cnt[inc])
        self.q[eng].append((ws, fn, tok, amt))
        return tok

    def finalize(self, stack):
        block = stack.enter_context(self.nc.Block())
        for eng in ENGS:
            ops = self.q[eng]
            if not ops:
                continue

            def body(e, ops=ops):
                for ws, fn, tok, amt in ops:
                    for (s, v) in ws:
                        e.wait_ge(self.sems[s], v)
                    ins = fn(e)
                    if tok is not None:
                        ins.then_inc(self.sems[tok[0]], amt)
            getattr(block, eng)(body)


class Bank:
    def __init__(self, t):
        self.t = t
        self.wtok = None
        self.rtok = []


class Slot:
    def __init__(self, buf):
        self.buf = buf
        self.wtok = None
        self.rtok = None


def build(nc, npass=NPASS, maxphase=6, dbg=(), sub=9, p2cut=9):
    dt = nc.dram_tensor
    x_h = dt("x", [BPC, S, D], F32, kind="ExternalInput").ap()
    ins = {}
    for nm, shp in (("wq", [D, D]), ("wk", [D, D]), ("wv", [D, D]), ("wo", [D, D]),
                    ("proj", [128, 128]), ("w1", [D, FF]), ("w2", [FF, D]),
                    ("b1c", [128, FF // 128]), ("b2c", [128, DC]),
                    ("g1c", [128, DC]), ("be1c", [128, DC]),
                    ("g2c", [128, DC]), ("be2c", [128, DC]),
                    ("WgT", [D, NE]), ("WeGT", [NE, D, HID]), ("WeUT", [NE, D, HID]),
                    ("WeDT", [NE, HID, D]), ("WsGT", [D, SH]), ("WsUT", [D, SH]),
                    ("WsDT", [SH, D]), ("WsgT", [128, DC]),
                    ("idf", [128, 128]), ("maskT", [128, 512]),
                    ("onesf", [128, 128])):
        ins[nm] = dt(nm, shp, F32, kind="ExternalInput").ap()
    ins["onehot"] = dt("onehot", [NE, NE * 128], BF16, kind="ExternalInput").ap()
    onesb_h = dt("onesb", [128, 128], BF16, kind="ExternalInput").ap()
    y_h = dt("y", [BPC, S, D], F32, kind="ExternalOutput").ap()

    stack = ExitStack()
    with stack:
        sb = lambda n, shp, d=BF16: stack.enter_context(nc.sbuf_tensor(n, shp, d))
        idf = sb("s_idf", [128, 128], F32)
        maskT = sb("s_maskT", [128, 512], F32)
        onehot = sb("s_onehot", [NE, NE * 128], BF16)
        onesb = sb("s_onesb", [128, 128], BF16)
        onesf = sb("s_onesf", [128, 128], F32)
        proj = sb("projb", [128, 128], BF16)
        WgTs = sb("WgTs", [128, DC * NE], BF16)
        WsgTs = sb("WsgTs", [128, DC], BF16)
        b1c = sb("s_b1c", [128, FF // 128], F32)
        b2c = sb("s_b2c", [128, DC], F32)
        g1c = sb("s_g1c", [128, DC], F32)
        be1c = sb("s_be1c", [128, DC], F32)
        g2c = sb("s_g2c", [128, DC], F32)
        be2c = sb("s_be2c", [128, DC], F32)
        kvf = sb("kvf", [64, H * DH], F32)
        ksf = sb("ksf", [64, H], F32)
        kvb = sb("kvb", [128, H * DH], BF16)
        ksb = sb("ksb", [128, H], BF16)
        xn = sb("xn", [128, DC * W])
        hbuf = sb("hbuf", [128, DC * W])
        abuf = sb("abuf", [128, DC * W])
        ybuf = sb("ybuf", [128, DC * W])
        big4 = sb("big4", [128, 4 * DC * 512])
        qraw = sb("qraw", [128, DC * 512])
        sigB = sb("sigB", [128, W], BF16)
        wT = sb("wT", [NE, W], BF16)
        w_tm = sb("w_tm", [128, NTT * NE], F32)
        smallb = sb("smallb", [128, 2048], BF16)
        stats = sb("stats", [128, 64], F32)
        rbt = sb("rbt", [64, 1024], F32)
        scr = sb("scr", [128, 6208], F32)
        wslot = [sb(f"ws{i}", [128, 4096]) for i in range(6)]

        ps = [stack.enter_context(nc.psum_tensor(f"p{i}", [128, 512], F32))
              for i in range(8)]
        P = Prog(nc, stack)
        banks = [Bank(t) for t in ps]
        slots = [Slot(t) for t in wslot]

        xn_v = xn[:, :].rearrange("q (c w) -> q c w", w=W)
        h_v = hbuf[:, :].rearrange("q (c w) -> q c w", w=W)
        a_v = abuf[:, :].rearrange("q (c w) -> q c w", w=W)
        y_v = ybuf[:, :].rearrange("q (c w) -> q c w", w=W)

        def syncdma(dst, src, waits=()):
            return P.do("sync", lambda e, d=dst, s=src: e.dma_start(d, s),
                        waits=waits, inc="xl", amt=16)

        def storedma(dst, src, waits=()):
            return P.do("sync", lambda e, d=dst, s=src: e.dma_start(d, s),
                        waits=waits, inc="so", amt=16)

        def gpdma(dst, src, waits=()):
            return P.do("gpsimd", lambda e, d=dst, s=src: e.dma_start(d, s),
                        waits=waits, inc="wl", amt=16)

        def load_slot(sl, src):
            sl.wtok = gpdma(sl.buf[:, :], src, waits=[sl.rtok])
            sl.rtok = None
            return sl.wtok

        rot = {"b": 0}

        def nextbank():
            i = rot["b"]
            rot["b"] = (i + 1) % 6
            return banks[i]

        def mm_group(bank, mms):
            n = len(mms)
            tok = None
            for i, item in enumerate(mms):
                (out, l, r, ws), tp = (item[:4], item[4] if len(item) > 4 else None)
                w = list(ws)
                if i == 0:
                    w.extend(bank.rtok)

                def f(e, out=out, l=l, r=r, i=i, n=n, tp=tp):
                    return e.matmul(out, l, r, start=(i == 0), stop=(i == n - 1),
                                    tile_position=tp)
                tok = P.do("tensor", f, waits=w,
                           inc=("pe" if i == n - 1 else None))
            bank.wtok = tok
            return tok

        def tr_group(bank, items):
            tok = None
            for i, (out, inp, ws) in enumerate(items):
                w = list(ws)
                if i == 0:
                    w.extend(bank.rtok)

                def f(e, out=out, inp=inp):
                    return e.transpose(out, inp, idf[:, :])
                tok = P.do("tensor", f, waits=w,
                           inc=("pe" if i == len(items) - 1 else None))
            bank.wtok = tok
            return tok

        def consume(bank, eng, fn, waits=(), sem=None):
            sem = sem or ("dve" if eng == "vector" else "act")
            tok = P.do(eng, fn, waits=[bank.wtok] + list(waits), inc=sem)
            bank.rtok = [tok]
            return tok

        def penow():
            return ("pe", P.cnt["pe"])

        ctoks = [
            syncdma(idf[:, :], ins["idf"]),
            syncdma(maskT[:, :], ins["maskT"]),
            syncdma(onehot[:, :], ins["onehot"]),
            syncdma(onesb[:, :], onesb_h),
            syncdma(onesf[:, :], ins["onesf"]),
            syncdma(b1c[:, :], ins["b1c"]),
            syncdma(b2c[:, :], ins["b2c"]),
            syncdma(g1c[:, :], ins["g1c"]),
            syncdma(be1c[:, :], ins["be1c"]),
            syncdma(g2c[:, :], ins["g2c"]),
            syncdma(be2c[:, :], ins["be2c"]),
            gpdma(proj[:, :], ins["proj"]),
            gpdma(WgTs[:, :], ins["WgT"].rearrange("(c p) m -> p c m", p=128)),
            gpdma(WsgTs[:, :], ins["WsgT"]),
        ]
        ck = ctoks[-1]
        cs = ctoks[-4]
        t_eps = P.do("vector", lambda e: e.memset(stats[:, 63:64], EPS), inc="dve")

        wq_src = ins["wq"].rearrange("(c p) m -> p c m", p=128)
        wk_src = ins["wk"].rearrange("(c p) m -> p c m", p=128)
        wv_src = ins["wv"].rearrange("(c p) m -> p c m", p=128)
        wo_src = ins["wo"].rearrange("(c p) m -> p c m", p=128)
        w1_src = ins["w1"].rearrange("(c p) m -> p c m", p=128)
        w2_src = ins["w2"].rearrange("(c p) m -> p c m", p=128)
        WeG_src = [ins["WeGT"][e].rearrange("(c p) m -> p c m", p=128) for e in range(NE)]
        WeU_src = [ins["WeUT"][e].rearrange("(c p) m -> p c m", p=128) for e in range(NE)]
        WeD_src = [ins["WeDT"][e].rearrange("(c p) m -> p c m", p=128) for e in range(NE)]
        WsG_src = ins["WsGT"].rearrange("(c p) m -> p c m", p=128)
        WsU_src = ins["WsUT"].rearrange("(c p) m -> p c m", p=128)
        WsD_src = ins["WsDT"].rearrange("(c p) m -> p c m", p=128)

        dbgset = set(dbg)
        dbg_regs = {}

        def reg_dbg(name, ap, pdim, fdim, dtype):
            if name in dbgset and name not in dbg_regs:
                o = dt("dbg_" + name, [pdim, fdim], dtype, kind="ExternalOutput").ap()
                dbg_regs[name] = (ap, o)

        last_x_use = [None, None]
        outst_free = [None, None]
        state_tok = {0: None, 1: None}
        bar = {"pe": None, "dve": None, "act": None}

        for p in range(npass):
            seq, half = p // 2, p % 2
            row0 = half * W
            pbar = [bar["pe"], bar["dve"], bar["act"]]

            if half == 0:
                P.do("vector", lambda e: e.memset(kvf[:, :], 0.0), waits=pbar, inc="dve")
                P.do("vector", lambda e: e.memset(ksf[:, :], 0.0), inc="dve")
                P.do("vector", lambda e: e.memset(kvb[:, :], 0.0), inc="dve")
                z4 = P.do("vector", lambda e: e.memset(ksb[:, :], 0.0), inc="dve")
                state_tok[0] = z4
                state_tok[1] = z4
            xin = [scr[:, 2048:3072], scr[:, 3072:4096]]
            xload = [None] * NTT

            xt_tok = [None] * NTT
            ht_tok = [None] * NTT
            prev_trA = None
            prev_trB = None
            for tt in range(NTT):
                xload[tt] = syncdma(xin[tt % 2],
                                    x_h[seq, row0 + tt * 128: row0 + (tt + 1) * 128, :],
                                    waits=[last_x_use[tt % 2]])
                xt = xin[tt % 2]
                P.do("vector", lambda e, xt=xt: e.bn_stats(
                    stats[:, 16:22], xt[:, 0:512]),
                    waits=[xload[tt]] + pbar, inc="dve")
                P.do("vector", lambda e, xt=xt: e.bn_stats(
                    stats[:, 22:28], xt[:, 512:1024]), inc="dve")
                t4b = P.do("scalar", lambda e: e.copy(
                    stats[:, 40:52], stats[:, 16:28]),
                    waits=[("dve", P.cnt["dve"])], inc="act")
                t5 = P.do("vector", lambda e: e.bn_aggr(
                    stats[:, 0:2], stats[:, 40:52]), waits=[t4b], inc="dve")
                t5b = P.do("scalar", lambda e: e.copy(
                    stats[:, 2:4], stats[:, 0:2]), waits=[t5], inc="act")
                t6 = P.do("scalar", lambda e: e.activation(
                    stats[:, 5:6], stats[:, 3:4], AF.Ln, bias=stats[:, 63:64]),
                    waits=[t5b, t_eps], inc="act")
                t7 = P.do("scalar", lambda e: e.activation(
                    stats[:, 6:7], stats[:, 5:6], AF.Exp, scale=-0.5),
                    waits=[t6], inc="act")
                t8 = P.do("vector", lambda e, xt=xt: e.tensor_scalar(
                    scr[:, 0:1024], xt, stats[:, 2:3], stats[:, 6:7],
                    ALU.subtract, ALU.mult), waits=[t7, prev_trA], inc="dve")
                P.do("vector", lambda e: e.bn_stats(
                    stats[:, 16:22], scr[:, 0:512]), waits=[t8], inc="dve")
                P.do("vector", lambda e: e.bn_stats(
                    stats[:, 22:28], scr[:, 512:1024]), inc="dve")
                u4b = P.do("scalar", lambda e: e.copy(
                    stats[:, 40:52], stats[:, 16:28]),
                    waits=[("dve", P.cnt["dve"])], inc="act")
                u5 = P.do("vector", lambda e: e.bn_aggr(
                    stats[:, 8:10], stats[:, 40:52]), waits=[u4b], inc="dve")
                u5b = P.do("scalar", lambda e: e.copy(
                    stats[:, 10:12], stats[:, 8:10]), waits=[u5], inc="act")
                u6 = P.do("scalar", lambda e: e.activation(
                    stats[:, 12:13], stats[:, 11:12], AF.Ln, bias=stats[:, 63:64]),
                    waits=[u5b], inc="act")
                u7 = P.do("scalar", lambda e: e.activation(
                    stats[:, 13:14], stats[:, 12:13], AF.Exp, scale=-0.5),
                    waits=[u6], inc="act")
                u8 = P.do("vector", lambda e: e.tensor_scalar(
                    scr[:, 1024:2048], scr[:, 0:1024], stats[:, 10:11], stats[:, 13:14],
                    ALU.subtract, ALU.mult), waits=[u7, prev_trB], inc="dve")
                last_x_use[tt % 2] = t8
                for g in range(2):
                    bA = nextbank()
                    tr_group(bA, [(bA.t[:, j * 128:(j + 1) * 128],
                                   scr[:, (g * 4 + j) * 128:(g * 4 + j + 1) * 128],
                                   [t8, cs]) for j in range(4)])
                    prev_trA = bA.wtok
                    tokx = consume(bA, "scalar", lambda e, bA=bA, g=g, tt=tt: e.copy(
                        xn_v[:, g * 4:(g + 1) * 4, tt * 128:(tt + 1) * 128],
                        bA.t[:, 0:512].rearrange("q (c w) -> q c w", w=128)))
                    bB = nextbank()
                    tr_group(bB, [(bB.t[:, j * 128:(j + 1) * 128],
                                   scr[:, 1024 + (g * 4 + j) * 128:1024 + (g * 4 + j + 1) * 128],
                                   [u8]) for j in range(4)])
                    prev_trB = bB.wtok
                    tokh = None
                    for j in range(4):
                        d = g * 4 + j
                        tokh = P.do("vector", lambda e, bB=bB, j=j, d=d, tt=tt: e.tensor_scalar(
                            hbuf[:, d * W + tt * 128: d * W + (tt + 1) * 128],
                            bB.t[:, j * 128:(j + 1) * 128],
                            g1c[:, d:d + 1], be1c[:, d:d + 1], ALU.mult, ALU.add),
                            waits=[bB.wtok, cs], inc="dve")
                    bB.rtok = [tokh]
                    if g == 1:
                        xt_tok[tt] = tokx
                        ht_tok[tt] = tokh

            aF_toks = []
            for tt in range(NTT):
                bG = nextbank()
                mm_group(bG, [(bG.t[:, 0:NE],
                               xn[:, d * W + tt * 128: d * W + (tt + 1) * 128],
                               WgTs[:, d * NE:(d + 1) * NE],
                               [xt_tok[tt], ck]) for d in range(DC)])
                gsc = scr[:, 6144:6160]
                m1s = scr[:, 6160:6176]
                g2s = scr[:, 6176:6192]
                m2s = scr[:, 6192:6208]
                a1 = consume(bG, "vector", lambda e, bG=bG: e.tensor_reduce(
                    stats[:, 32:33], bG.t[:, 0:NE], AX.X, ALU.max))
                a1b = P.do("scalar", lambda e: e.copy(stats[:, 40:41], stats[:, 32:33]),
                           waits=[a1], inc="act")
                a2 = P.do("vector", lambda e, bG=bG, gsc=gsc: e.tensor_scalar(
                    gsc, bG.t[:, 0:NE], stats[:, 40:41], None, ALU.subtract),
                    waits=[a1b], inc="dve")
                bG.rtok = [a2]
                a3 = P.do("scalar", lambda e, gsc=gsc: e.activation(gsc, gsc, AF.Exp),
                          waits=[a2], inc="act")
                a4 = P.do("vector", lambda e, gsc=gsc: e.tensor_reduce(
                    stats[:, 33:34], gsc, AX.X, ALU.add), waits=[a3], inc="dve")
                a4b = P.do("scalar", lambda e: e.copy(stats[:, 41:42], stats[:, 33:34]),
                           waits=[a4], inc="act")
                a5 = P.do("vector", lambda e: e.reciprocal(stats[:, 34:35], stats[:, 41:42]),
                          waits=[a4b], inc="dve")
                a5b = P.do("scalar", lambda e: e.copy(stats[:, 42:43], stats[:, 34:35]),
                           waits=[a5], inc="act")
                a6 = P.do("vector", lambda e, gsc=gsc: e.tensor_scalar(
                    gsc, gsc, stats[:, 42:43], None, ALU.mult), waits=[a5b], inc="dve")
                a7 = P.do("vector", lambda e, gsc=gsc: e.tensor_reduce(
                    stats[:, 35:36], gsc, AX.X, ALU.max), waits=[a6], inc="dve")
                a7b = P.do("scalar", lambda e: e.copy(stats[:, 43:44], stats[:, 35:36]),
                           waits=[a7], inc="act")
                a8 = P.do("vector", lambda e, gsc=gsc, m1s=m1s: e.tensor_scalar(
                    m1s, gsc, stats[:, 43:44], None, ALU.is_ge), waits=[a7b], inc="dve")
                a9 = P.do("vector", lambda e, gsc=gsc, m1s=m1s, g2s=g2s: e.scalar_tensor_tensor(
                    g2s, m1s, -2.0, gsc, ALU.mult, ALU.add), waits=[a8], inc="dve")
                aA = P.do("vector", lambda e, g2s=g2s: e.tensor_reduce(
                    stats[:, 36:37], g2s, AX.X, ALU.max), waits=[a9], inc="dve")
                aAb = P.do("scalar", lambda e: e.copy(stats[:, 44:45], stats[:, 36:37]),
                           waits=[aA], inc="act")
                aB = P.do("vector", lambda e, g2s=g2s, m2s=m2s: e.tensor_scalar(
                    m2s, g2s, stats[:, 44:45], None, ALU.is_ge), waits=[aAb], inc="dve")
                aC = P.do("vector", lambda e: e.tensor_tensor(
                    stats[:, 37:38], stats[:, 43:44], stats[:, 44:45], ALU.add),
                    waits=[aB], inc="dve")
                aD = P.do("vector", lambda e: e.reciprocal(stats[:, 38:39], stats[:, 37:38]),
                          waits=[aC], inc="dve")
                aDb = P.do("scalar", lambda e: e.copy(stats[:, 45:46], stats[:, 38:39]),
                           waits=[aD], inc="act")
                aE = P.do("vector", lambda e, m1s=m1s, m2s=m2s: e.tensor_tensor(
                    m1s, m1s, m2s, ALU.add), waits=[aB], inc="dve")
                aF = P.do("vector", lambda e, gsc=gsc, m1s=m1s, tt=tt: e.scalar_tensor_tensor(
                    w_tm[:, tt * NE:(tt + 1) * NE], gsc, stats[:, 45:46], m1s,
                    ALU.mult, ALU.mult), waits=[aDb, aE], inc="dve")
                aF_toks.append(aF)
            for tt in range(NTT):
                bT = nextbank()
                tr_group(bT, [(bT.t[0:NE, 0:128], w_tm[:, tt * NE:(tt + 1) * NE],
                               [aF_toks[tt], cs])])
                consume(bT, "scalar", lambda e, bT=bT, tt=tt: e.copy(
                    wT[0:NE, tt * 128:(tt + 1) * 128], bT.t[0:NE, 0:128]))
            wT_tok = ("act", P.cnt["act"])

            for th in range(2):
                bS = nextbank()
                mm_group(bS, [(bS.t[0:1, 0:512], WsgTs[:, d:d + 1],
                               xn_v[:, d, th * 512:(th + 1) * 512],
                               [xt_tok[th * 4 + 3], ck]) for d in range(DC)])
                sv = consume(bS, "scalar", lambda e, bS=bS: e.activation(
                    smallb[0:1, 0:512], bS.t[0:1, 0:512], AF.Sigmoid))
                bS2 = nextbank()
                mm_group(bS2, [(bS2.t[:, 0:512], onesb[0:1, :],
                                smallb[0:1, 0:512], [sv])])
                consume(bS2, "vector", lambda e, bS2=bS2, th=th: e.tensor_copy(
                    sigB[:, th * 512:(th + 1) * 512], bS2.t[:, 0:512]))
            sig_done = ("dve", P.cnt["dve"])

            reg_dbg("xn", xn[:, :], 128, DC * W, BF16)
            reg_dbg("h", hbuf[:, :], 128, DC * W, BF16)
            reg_dbg("w_tm", w_tm[:, :], 128, NTT * NE, F32)
            reg_dbg("sigB", sigB[:, :], 128, W, F32)
            if maxphase < 2:
                bar = {"pe": penow(), "dve": ("dve", P.cnt["dve"]), "act": ("act", P.cnt["act"])}
                continue

            q_of, k_of, v_of, kt_of = 0, DC * 512, 2 * DC * 512, 3 * DC * 512
            atw = {}
            for i, (nm, src) in enumerate((("wq", wq_src), ("wk", wk_src), ("wv", wv_src))):
                for hf in range(2):
                    sl = slots[i * 2 + hf]
                    load_slot(sl, src[:, :, hf * 512:(hf + 1) * 512])
                    atw[(nm, hf)] = sl

            qkv_free = None
            smr = [None, None]
            drr = None
            rbr = [None, None]
            attn_last = [None, None]
            scan_i = 0

            for st in range(2):
                phi_q_done = None
                phi_last = {}
                rev_k = None
                prev_elu = None
                projs = () if p2cut == 0 else ((("wq", q_of),) if p2cut in (1, 11, 12, 13, 14) else (("wq", q_of), ("wk", k_of)))
                for nm, dstof in projs:
                    rev_toks = []
                    for c in range(DC):
                        bR = nextbank()
                        mm_group(bR, [(bR.t[:, 0:512],
                                       atw[(nm, c // 4)].buf[:, d * 512 + (c % 4) * 128: d * 512 + (c % 4) * 128 + 128],
                                       h_v[:, d, st * 512:(st + 1) * 512],
                                       [ht_tok[st * 4 + 3], atw[(nm, c // 4)].wtok])
                                      for d in range(DC)])
                        rev = consume(bR, "vector", lambda e, bR=bR, c=c: e.tensor_copy(
                            qraw[:, c * 512:(c + 1) * 512], bR.t[:, 0:512]),
                            waits=[qkv_free, phi_q_done])
                        rev_toks.append(rev)
                        if p2cut == 11:
                            phi_last[nm] = rev
                            continue
                        bP = nextbank()
                        mm_group(bP, [(bP.t[:, 0:512], proj[:, :],
                                       qraw[:, c * 512:(c + 1) * 512], [rev, ck])])
                        if p2cut == 13:
                            ev0 = consume(bP, "vector", lambda e, bP=bP, dstof=dstof, c=c:
                                e.tensor_copy(big4[:, dstof + c * 512: dstof + (c + 1) * 512],
                                              bP.t[:, 0:512]), waits=[qkv_free])
                            prev_elu = ev0
                            phi_last[nm] = ev0
                            continue
                        e1 = P.do("vector", lambda e, bP=bP: e.tensor_scalar(
                            scr[:, 0:512], bP.t[:, 0:512], 0.0, None, ALU.max),
                            waits=[bP.wtok, prev_elu], inc="dve")
                        e2 = P.do("vector", lambda e, bP=bP: e.tensor_scalar(
                            scr[:, 1024:1536], bP.t[:, 0:512], 0.0, None, ALU.min),
                            waits=[bP.wtok], inc="dve")
                        bP.rtok = [e2]
                        e3 = P.do("scalar", lambda e: e.activation(
                            scr[:, 1024:1536], scr[:, 1024:1536], AF.Exp),
                            waits=[e2], inc="act")
                        e4 = P.do("vector", lambda e, dstof=dstof, c=c: e.tensor_tensor(
                            big4[:, dstof + c * 512: dstof + (c + 1) * 512],
                            scr[:, 0:512], scr[:, 1024:1536], ALU.add),
                            waits=[e1, e3, qkv_free], inc="dve")
                        prev_elu = e4
                        phi_last[nm] = e4
                    if nm == "wq":
                        phi_q_done = penow()
                    else:
                        rev_k = rev_toks
                ktm_toks = []
                for tc in range(4 if 3 <= p2cut <= 9 else 0):
                    for hg in range(2):
                        bK = nextbank()
                        mms = []
                        for pp in range(4):
                            pair = hg * 4 + pp
                            mms.append((bK.t[:, pp * 128:(pp + 1) * 128],
                                        qraw[:, pair * 512 + tc * 128: pair * 512 + (tc + 1) * 128],
                                        proj[:, :],
                                        [rev_k[pair], ck]))
                        mm_group(bK, mms)
                        f1 = P.do("vector", lambda e, bK=bK: e.tensor_scalar(
                            scr[:, 0:512], bK.t[:, 0:512], 0.0, None, ALU.max),
                            waits=[bK.wtok, prev_elu], inc="dve")
                        f2 = P.do("vector", lambda e, bK=bK: e.tensor_scalar(
                            scr[:, 1024:1536], bK.t[:, 0:512], 0.0, None, ALU.min),
                            waits=[bK.wtok], inc="dve")
                        bK.rtok = [f2]
                        f3 = P.do("scalar", lambda e: e.activation(
                            scr[:, 1024:1536], scr[:, 1024:1536], AF.Exp),
                            waits=[f2], inc="act")
                        f4 = P.do("vector", lambda e, tc=tc, hg=hg: e.tensor_tensor(
                            big4[:, kt_of + tc * 1024 + hg * 512: kt_of + tc * 1024 + (hg + 1) * 512],
                            scr[:, 0:512], scr[:, 1024:1536], ALU.add),
                            waits=[f1, f3, qkv_free], inc="dve")
                        prev_elu = f4
                        ktm_toks.append(f4)
                v_toks = {}
                for tc in range(4 if 4 <= p2cut <= 9 else 0):
                    for cf in range(2):
                        bV = nextbank()
                        mm_group(bV, [(bV.t[:, 0:512],
                                       h_v[:, d, st * 512 + tc * 128: st * 512 + (tc + 1) * 128],
                                       atw[("wv", cf)].buf[:, d * 512:(d + 1) * 512],
                                       [ht_tok[st * 4 + tc], atw[("wv", cf)].wtok])
                                      for d in range(DC)])
                        v_toks[(tc, cf)] = consume(bV, "scalar",
                            lambda e, bV=bV, tc=tc, cf=cf: e.copy(
                                big4[:, v_of + tc * 1024 + cf * 512: v_of + tc * 1024 + (cf + 1) * 512],
                                bV.t[:, 0:512]), waits=[qkv_free])
                if st == 1 and p2cut >= 1:
                    for key in atw:
                        atw[key].rtok = penow()

                if sub < 2:
                    qkv_free = penow()
                    attn_last[st] = ("dve", P.cnt["dve"])
                    continue
                for tc in range(4):
                    for p_ in range(2):
                        i2 = scan_i % 2
                        base = p_ * 64
                        S_mw = smallb[:, i2 * 1024:(i2 + 1) * 1024]
                        qfs = [big4[base:base + 64,
                                    q_of + i * 512 + tc * 128: q_of + i * 512 + (tc + 1) * 128]
                               for i in range(8)]
                        kfs = [big4[base:base + 64,
                                    k_of + i * 512 + tc * 128: k_of + i * 512 + (tc + 1) * 128]
                               for i in range(8)]
                        vts = [big4[:, v_of + tc * 1024 + (p_ + 2 * i) * 64:
                                    v_of + tc * 1024 + (p_ + 2 * i + 1) * 64] for i in range(8)]
                        kts = [big4[:, kt_of + tc * 1024 + (p_ + 2 * i) * 64:
                                    kt_of + tc * 1024 + (p_ + 2 * i + 1) * 64] for i in range(8)]
                        bS0 = nextbank(); bS1 = nextbank()
                        mm_group(bS0, [(bS0.t[:, (i % 4) * 128:(i % 4 + 1) * 128],
                                        kfs[i], qfs[i],
                                        [phi_last["wk"], phi_last["wq"]]) for i in range(4)])
                        mm_group(bS1, [(bS1.t[:, (i % 4) * 128:(i % 4 + 1) * 128],
                                        kfs[i + 4], qfs[i + 4], []) for i in range(4)])
                        mk0 = consume(bS0, "vector", lambda e, bS0=bS0, S_mw=S_mw: e.tensor_tensor(
                            S_mw[:, 0:512], bS0.t[:, 0:512], maskT[:, :], ALU.mult),
                            waits=[smr[i2], cs])
                        mk1 = consume(bS1, "vector", lambda e, bS1=bS1, S_mw=S_mw: e.tensor_tensor(
                            S_mw[:, 512:1024], bS1.t[:, 0:512], maskT[:, :], ALU.mult))
                        bN0 = nextbank(); bN1 = nextbank()
                        mms0, mms1 = [], []
                        for i in range(8):
                            bb, mm, mkt = (bN0, mms0, mk0) if i < 4 else (bN1, mms1, mk1)
                            reg = bb.t[0:64, (i % 4) * 128:(i % 4 + 1) * 128]
                            mm.append((reg, vts[i],
                                       S_mw[:, i * 128:(i + 1) * 128],
                                       [mkt, v_toks[(tc, i // 4)]]))
                            mm.append((reg,
                                       kvb[base:base + 64, p_ * 512 + i * 64: p_ * 512 + (i + 1) * 64],
                                       qfs[i], [state_tok[p_]]))
                        mm_group(bN0, mms0)
                        mm_group(bN1, mms1)
                        bD0 = nextbank(); bD1 = nextbank()
                        mmd0, mmd1 = [], []
                        for i in range(8):
                            bb, mm = (bD0, mmd0) if i < 4 else (bD1, mmd1)
                            reg = bb.t[0:1, (i % 4) * 128:(i % 4 + 1) * 128]
                            mm.append((reg, onesb[:, 0:1], S_mw[:, i * 128:(i + 1) * 128], []))
                            mm.append((reg, ksb[base:base + 64, p_ * 8 + i: p_ * 8 + i + 1],
                                       qfs[i], [state_tok[p_]]))
                        mm_group(bD0, mmd0)
                        mm_group(bD1, mmd1)
                        dd1 = consume(bD0, "vector", lambda e, bD0=bD0: e.tensor_scalar(
                            scr[0:1, 5120:5632], bD0.t[0:1, 0:512], 1e-6, None, ALU.add),
                            waits=[drr])
                        dd2 = consume(bD1, "vector", lambda e, bD1=bD1: e.tensor_scalar(
                            scr[0:1, 5632:6144], bD1.t[0:1, 0:512], 1e-6, None, ALU.add))
                        dd3 = P.do("vector", lambda e: e.reciprocal(scr[0:1, 5120:6144], scr[0:1, 5120:6144]),
                                   inc="dve")
                        dd4 = P.do("scalar", lambda e: e.copy(scr[0:1, 4096:5120], scr[0:1, 5120:6144]),
                                   waits=[dd3, drr], inc="act")
                        mm_group(bS0, [(bS0.t[0:64, 0:512], onesf[0:1, 0:64],
                                        scr[0:1, 4096:4608], [dd4])])
                        mm_group(bS1, [(bS1.t[0:64, 0:512], onesf[0:1, 0:64],
                                        scr[0:1, 4608:5120], [])])
                        drr = bS1.wtok
                        rb0 = consume(bS0, "scalar", lambda e, bS0=bS0: e.copy(
                            rbt[0:64, 0:512], bS0.t[0:64, 0:512]), waits=[rbr[0]])
                        rb1 = consume(bS1, "scalar", lambda e, bS1=bS1: e.copy(
                            rbt[0:64, 512:1024], bS1.t[0:64, 0:512]), waits=[rbr[1]])
                        a0 = P.do("vector", lambda e, bN0=bN0, base=base, tc=tc, st=st:
                            e.tensor_tensor(
                                a_v[base:base + 64, 0:4,
                                    st * 512 + tc * 128: st * 512 + (tc + 1) * 128],
                                bN0.t[0:64, 0:512].rearrange("q (c w) -> q c w", w=128),
                                rbt[0:64, 0:512].rearrange("q (c w) -> q c w", w=128),
                                ALU.mult),
                            waits=[bN0.wtok, rb0], inc="dve")
                        a1 = P.do("vector", lambda e, bN1=bN1, base=base, tc=tc, st=st:
                            e.tensor_tensor(
                                a_v[base:base + 64, 4:8,
                                    st * 512 + tc * 128: st * 512 + (tc + 1) * 128],
                                bN1.t[0:64, 0:512].rearrange("q (c w) -> q c w", w=128),
                                rbt[0:64, 512:1024].rearrange("q (c w) -> q c w", w=128),
                                ALU.mult),
                            waits=[bN1.wtok, rb1], inc="dve")
                        bN0.rtok = [a0]
                        bN1.rtok = [a1]
                        rbr = [a0, a1]
                        attn_last[st] = a1
                        smr[i2] = bD1.wtok
                        bKV, bKS = banks[6], banks[7]
                        mm_group(bKV, [(bKV.t[0:64, i * 64:(i + 1) * 64], kts[i], vts[i],
                                        [ktm_toks[tc * 2 + i // 4], v_toks[(tc, i // 4)]])
                                       for i in range(8)])
                        mm_group(bKS, [(bKS.t[0:64, i:i + 1], kts[i], onesb[:, 0:1], [])
                                       for i in range(8)])
                        kva = consume(bKV, "vector", lambda e, bKV=bKV, p_=p_: e.tensor_tensor(
                            kvf[:, p_ * 512:(p_ + 1) * 512], kvf[:, p_ * 512:(p_ + 1) * 512],
                            bKV.t[0:64, 0:512], ALU.add))
                        ksa = consume(bKS, "vector", lambda e, bKS=bKS, p_=p_: e.tensor_tensor(
                            ksf[:, p_ * 8:(p_ + 1) * 8], ksf[:, p_ * 8:(p_ + 1) * 8],
                            bKS.t[0:64, 0:8], ALU.add))
                        P.do("scalar", lambda e, base=base, p_=p_: e.copy(
                            kvb[base:base + 64, p_ * 512:(p_ + 1) * 512],
                            kvf[:, p_ * 512:(p_ + 1) * 512]), waits=[kva], inc="act")
                        c2 = P.do("scalar", lambda e, base=base, p_=p_: e.copy(
                            ksb[base:base + 64, p_ * 8:(p_ + 1) * 8],
                            ksf[:, p_ * 8:(p_ + 1) * 8]), waits=[ksa], inc="act")
                        state_tok[p_] = c2
                        scan_i += 1
                qkv_free = penow()

            reg_dbg("big4", big4[:, :], 128, 4 * DC * 512, BF16)
            if sub < 3:
                bar = {"pe": penow(), "dve": ("dve", P.cnt["dve"]), "act": ("act", P.cnt["act"])}
                continue
            for hf in range(2):
                sl = slots[hf]
                load_slot(sl, wo_src[:, :, hf * 512:(hf + 1) * 512])
                atw[("wo", hf)] = sl
            x1_done = None
            for st in range(2):
                for c in range(DC):
                    bW = nextbank()
                    mm_group(bW, [(bW.t[:, 0:512],
                                   atw[("wo", c // 4)].buf[:, d * 512 + (c % 4) * 128: d * 512 + (c % 4) * 128 + 128],
                                   a_v[:, d, st * 512:(st + 1) * 512],
                                   [attn_last[st], atw[("wo", c // 4)].wtok])
                                  for d in range(DC)])
                    x1_done = consume(bW, "vector", lambda e, bW=bW, c=c, st=st:
                        e.tensor_tensor(
                            y_v[:, c, st * 512:(st + 1) * 512],
                            bW.t[:, 0:512],
                            xn_v[:, c, st * 512:(st + 1) * 512], ALU.add))
            for hf in range(2):
                atw[("wo", hf)].rtok = penow()

            reg_dbg("attn", abuf[:, :], 128, DC * W, BF16)
            reg_dbg("x1", ybuf[:, :], 128, DC * W, BF16)
            if maxphase < 3:
                bar = {"pe": penow(), "dve": ("dve", P.cnt["dve"]), "act": ("act", P.cnt["act"])}
                continue

            h2_guard = penow()
            h2_toks = [None, None]
            for th in range(2):
                bL = nextbank()
                mm_group(bL, [(bL.t[0:1, 0:512], onesb[:, 0:1],
                               y_v[:, d, th * 512:(th + 1) * 512], [x1_done])
                              for d in range(DC)])
                bQ = nextbank()
                sq_prev_mm = None
                for dg in range(4):
                    sq_toks = []
                    for dl in range(2):
                        d = dg * 2 + dl
                        sqb = smallb[:, dl * 512:(dl + 1) * 512]
                        sqt = P.do("scalar", lambda e, d=d, th=th, sqb=sqb: e.activation(
                            sqb, y_v[:, d, th * 512:(th + 1) * 512], AF.Square),
                            waits=[x1_done, sq_prev_mm], inc="act")
                        sq_toks.append((sqb, sqt))
                    for dl, (sqb, sqt) in enumerate(sq_toks):
                        ww = [sqt]
                        if dg == 0 and dl == 0:
                            ww.extend(bQ.rtok)

                        def f(e, sqb=sqb, dg=dg, dl=dl, bQ=bQ):
                            return e.matmul(bQ.t[0:1, 0:512], onesb[:, 0:1], sqb,
                                            start=(dg == 0 and dl == 0),
                                            stop=(dg == 3 and dl == 1))
                        tok = P.do("tensor", f, waits=ww,
                                   inc=("pe" if dl == 1 else None))
                    sq_prev_mm = tok
                bQ.wtok = tok
                m_ = scr[0:1, 4096:4608]
                msq = scr[0:1, 4608:5120]
                var = scr[0:1, 5120:5632]
                rstd = scr[0:1, 5120:5632]
                l1 = consume(bL, "vector", lambda e, bL=bL, m_=m_: e.tensor_scalar(
                    m_, bL.t[0:1, 0:512], 1.0 / D, None, ALU.mult))
                l2 = P.do("vector", lambda e, m_=m_, msq=msq: e.tensor_tensor(
                    msq, m_, m_, ALU.mult), waits=[l1], inc="dve")
                l3a = consume(bQ, "vector", lambda e, bQ=bQ, var=var: e.tensor_scalar(
                    var, bQ.t[0:1, 0:512], 1.0 / D, None, ALU.mult),
                    waits=[l2])
                l3 = P.do("vector", lambda e, var=var, msq=msq: e.tensor_tensor(
                    var, var, msq, ALU.subtract), inc="dve")
                l4 = P.do("scalar", lambda e, var=var: e.activation(
                    var, var, AF.Ln, bias=stats[0:1, 63:64]), waits=[l3], inc="act")
                l5 = P.do("scalar", lambda e, var=var, rstd=rstd: e.activation(
                    rstd, var, AF.Exp, scale=-0.5), waits=[l4], inc="act")
                l6 = P.do("vector", lambda e, m_=m_, rstd=rstd, msq=msq: e.tensor_tensor(
                    msq, m_, rstd, ALU.mult), waits=[l5], inc="dve")
                l7 = P.do("scalar", lambda e, rstd=rstd: e.copy(
                    smallb[0:1, 1024:1536], rstd), waits=[l5], inc="act")
                l8 = P.do("scalar", lambda e, msq=msq: e.copy(
                    smallb[0:1, 1536:2048], msq), waits=[l6], inc="act")
                bR1 = nextbank()
                mm_group(bR1, [(bR1.t[:, 0:512], onesb[0:1, :], smallb[0:1, 1024:1536], [l7])])
                bR2 = nextbank()
                mm_group(bR2, [(bR2.t[:, 0:512], onesb[0:1, :], smallb[0:1, 1536:2048], [l8])])
                htk = None
                for d in range(DC):
                    w1_ = P.do("vector", lambda e, bR1=bR1, d=d, th=th: e.tensor_tensor(
                        scr[:, 0:512], y_v[:, d, th * 512:(th + 1) * 512],
                        bR1.t[:, 0:512], ALU.mult),
                        waits=[bR1.wtok, x1_done], inc="dve")
                    w2_ = P.do("vector", lambda e, bR2=bR2: e.tensor_tensor(
                        scr[:, 1024:1536], scr[:, 0:512], bR2.t[:, 0:512], ALU.subtract),
                        waits=[bR2.wtok, w1_], inc="dve")
                    htk = P.do("vector", lambda e, d=d, th=th: e.tensor_scalar(
                        hbuf[:, d * W + th * 512: d * W + (th + 1) * 512],
                        scr[:, 1024:1536],
                        g2c[:, d:d + 1], be2c[:, d:d + 1], ALU.mult, ALU.add),
                        waits=[w2_, h2_guard, cs], inc="dve")
                bR1.rtok = [htk]
                bR2.rtok = [htk]
                h2_toks[th] = htk

            reg_dbg("h2", hbuf[:, :], 128, DC * W, BF16)
            gelu_guard = penow()
            s_done = None
            for st in range(2):
                gel_toks = []
                w1sl = [slots[i] for i in range(3)]
                w2sl = [slots[3 + i] for i in range(3)]
                for fb in range(3):
                    load_slot(w1sl[fb], w1_src[:, :, fb * 512:(fb + 1) * 512])
                for fb in range(3):
                    load_slot(w2sl[fb], w2_src[:, fb * 4:(fb + 1) * 4, :])
                for fc in range(32):
                    fb = fc // 4
                    sl = w1sl[fb % 3]
                    bF = nextbank()
                    mm_group(bF, [(bF.t[:, 0:512],
                                   sl.buf[:, d * 512 + (fc % 4) * 128: d * 512 + (fc % 4) * 128 + 128],
                                   h_v[:, d, st * 512:(st + 1) * 512],
                                   [h2_toks[st], sl.wtok]) for d in range(DC)])
                    if fc % 4 == 3:
                        sl.rtok = bF.wtok
                        if fb + 3 <= 7:
                            load_slot(sl, w1_src[:, :, (fb + 3) * 512:(fb + 4) * 512])
                    gel_toks.append(consume(bF, "scalar", lambda e, bF=bF, fc=fc: e.activation(
                        big4[:, fc * 512:(fc + 1) * 512], bF.t[:, 0:512], AF.Gelu,
                        bias=b1c[:, fc:fc + 1]), waits=[gelu_guard, cs]))
                for fc in range(32):
                    fb = fc // 4
                    sl = w2sl[fb % 3]
                    for dout in range(DC):
                        bk = banks[dout]
                        w = [gel_toks[fc], sl.wtok]
                        if fc == 0:
                            w.extend(bk.rtok)

                        def f(e, bk=bk, sl=sl, fc=fc, dout=dout):
                            return e.matmul(
                                bk.t[:, 0:512],
                                sl.buf[:, (fc % 4) * 1024 + dout * 128:(fc % 4) * 1024 + (dout + 1) * 128],
                                big4[:, fc * 512:(fc + 1) * 512],
                                start=(fc == 0), stop=(fc == 31))
                        tok = P.do("tensor", f, waits=w,
                                   inc=("pe" if dout == DC - 1 else None))
                    if fc == 31:
                        for dout in range(DC):
                            banks[dout].wtok = tok
                    if fc % 4 == 3:
                        sl.rtok = tok
                        if fb + 3 <= 7:
                            load_slot(sl, w2_src[:, (fb + 3) * 4:(fb + 4) * 4, :])
                for dout in range(DC):
                    bk = banks[dout]
                    s_done = consume(bk, "vector", lambda e, bk=bk, dout=dout, st=st:
                        e.scalar_tensor_tensor(
                            a_v[:, dout, st * 512:(st + 1) * 512],
                            bk.t[:, 0:512], b2c[:, dout:dout + 1],
                            y_v[:, dout, st * 512:(st + 1) * 512],
                            ALU.add, ALU.add), waits=[cs])
                gelu_guard = penow()
            reg_dbg("s", abuf[:, :], 128, DC * W, BF16)
            if maxphase < 4:
                bar = {"pe": penow(), "dve": ("dve", P.cnt["dve"]), "act": ("act", P.cnt["act"])}
                continue

            x1_free = s_done
            hall_guard = penow()
            y_done = None
            for batch in range(4):
                es = [batch * 4 + i for i in range(4)]
                gsl, usl, dsl = {}, {}, {}
                for i, e_ in enumerate(es[:3]):
                    gsl[e_] = slots[2 * i]
                    load_slot(slots[2 * i], WeG_src[e_])
                    usl[e_] = slots[2 * i + 1]
                    load_slot(slots[2 * i + 1], WeU_src[e_])
                gsl[es[3]], usl[es[3]] = slots[0], slots[1]
                dsl[es[0]], dsl[es[1]] = slots[2], slots[3]
                dsl[es[2]], dsl[es[3]] = slots[4], slots[5]
                a_done = {}
                for ei, e_ in enumerate(es):
                    wb = {0: banks[6], 1: banks[7]}
                    mm_group(wb[0], [(wb[0].t[:, 0:512],
                                      onehot[0:NE, e_ * 128:(e_ + 1) * 128],
                                      wT[0:NE, 0:512], [wT_tok, cs])])
                    mm_group(wb[1], [(wb[1].t[:, 0:512],
                                      onehot[0:NE, e_ * 128:(e_ + 1) * 128],
                                      wT[0:NE, 512:1024], [wT_tok])])
                    last_h = None
                    prev_t2 = None
                    for th in range(2):
                        for hc in range(4):
                            bG = nextbank()
                            mm_group(bG, [(bG.t[:, 0:512],
                                           gsl[e_].buf[:, d * 512 + hc * 128: d * 512 + hc * 128 + 128],
                                           xn_v[:, d, th * 512:(th + 1) * 512],
                                           [gsl[e_].wtok, xt_tok[th * 4 + 3]]) for d in range(DC)])
                            bU = nextbank()
                            mm_group(bU, [(bU.t[:, 0:512],
                                           usl[e_].buf[:, d * 512 + hc * 128: d * 512 + hc * 128 + 128],
                                           xn_v[:, d, th * 512:(th + 1) * 512],
                                           [usl[e_].wtok]) for d in range(DC)])
                            sg = consume(bG, "scalar", lambda e, bG=bG: e.activation(
                                scr[:, 0:512], bG.t[:, 0:512], AF.Silu), waits=[prev_t2])
                            t2 = consume(bU, "vector", lambda e, bU=bU: e.tensor_tensor(
                                scr[:, 1024:1536], scr[:, 0:512], bU.t[:, 0:512], ALU.mult),
                                waits=[sg])
                            prev_t2 = t2
                            last_h = P.do("vector", lambda e, wbb=wb[th], ei=ei, hc=hc, th=th:
                                e.tensor_tensor(
                                    big4[:, ei * 4096 + hc * 1024 + th * 512:
                                         ei * 4096 + hc * 1024 + (th + 1) * 512],
                                    scr[:, 1024:1536], wbb.t[:, 0:512], ALU.mult),
                                waits=[wb[th].wtok, t2, hall_guard, x1_free], inc="dve")
                            a_done[(ei, th, hc)] = last_h
                    wb[0].rtok = [last_h]
                    wb[1].rtok = [last_h]
                    gsl[e_].rtok = penow()
                    usl[e_].rtok = penow()
                    if ei == 0:
                        load_slot(slots[0], WeG_src[es[3]])
                        load_slot(slots[1], WeU_src[es[3]])
                    elif ei == 1:
                        load_slot(slots[2], WeD_src[es[0]])
                        load_slot(slots[3], WeD_src[es[1]])
                    elif ei == 2:
                        load_slot(slots[4], WeD_src[es[2]])
                        load_slot(slots[5], WeD_src[es[3]])
                for th in range(2):
                    for dout in range(DC):
                        bk = banks[dout]
                        idx = 0
                        tok = None
                        for ei, e_ in enumerate(es):
                            for hc in range(4):
                                w = [dsl[e_].wtok, a_done[(ei, th, hc)]]
                                if idx == 0:
                                    w.extend(bk.rtok)

                                def f(e, bk=bk, sl=dsl[e_], ei=ei, hc=hc, th=th, dout=dout,
                                      idx=idx):
                                    return e.matmul(
                                        bk.t[:, 0:512],
                                        sl.buf[:, hc * 1024 + dout * 128: hc * 1024 + (dout + 1) * 128],
                                        big4[:, ei * 4096 + hc * 1024 + th * 512:
                                             ei * 4096 + hc * 1024 + (th + 1) * 512],
                                        start=(idx == 0), stop=(idx == 15))
                                tok = P.do("tensor", f, waits=w,
                                           inc=("pe" if idx == 15 else None))
                                idx += 1
                        bk.wtok = tok
                        if batch == 0:
                            y_done = consume(bk, "vector",
                                lambda e, bk=bk, dout=dout, th=th: e.tensor_copy(
                                    y_v[:, dout, th * 512:(th + 1) * 512], bk.t[:, 0:512]),
                                waits=[x1_free])
                        else:
                            y_done = consume(bk, "vector",
                                lambda e, bk=bk, dout=dout, th=th: e.tensor_tensor(
                                    y_v[:, dout, th * 512:(th + 1) * 512],
                                    y_v[:, dout, th * 512:(th + 1) * 512],
                                    bk.t[:, 0:512], ALU.add))
                for e_ in es:
                    dsl[e_].rtok = penow()
                hall_guard = penow()
            reg_dbg("ymoe", ybuf[:, :], 128, DC * W, BF16)
            if maxphase < 5:
                bar = {"pe": penow(), "dve": ("dve", P.cnt["dve"]), "act": ("act", P.cnt["act"])}
                continue

            hb_guard = penow()
            prev_hbm = None
            hb_dst = [hbuf, big4]
            sG, sU, sD = {}, {}, {}
            for j in range(3):
                sG[j] = slots[2 * j]
                load_slot(slots[2 * j], WsG_src[:, :, j * 512:(j + 1) * 512])
                sU[j] = slots[2 * j + 1]
                load_slot(slots[2 * j + 1], WsU_src[:, :, j * 512:(j + 1) * 512])
            sG[3], sU[3] = slots[0], slots[1]
            sD[0], sD[1], sD[2], sD[3] = slots[2], slots[3], slots[4], slots[5]
            hb_toks = {}
            for j in range(4):
                for hcl in range(4):
                    hc = j * 4 + hcl
                    for th in range(2):
                        bG = nextbank()
                        mm_group(bG, [(bG.t[:, 0:512],
                                       sG[j].buf[:, d * 512 + hcl * 128: d * 512 + hcl * 128 + 128],
                                       xn_v[:, d, th * 512:(th + 1) * 512],
                                       [sG[j].wtok, xt_tok[th * 4 + 3]]) for d in range(DC)])
                        bU = nextbank()
                        mm_group(bU, [(bU.t[:, 0:512],
                                       sU[j].buf[:, d * 512 + hcl * 128: d * 512 + hcl * 128 + 128],
                                       xn_v[:, d, th * 512:(th + 1) * 512],
                                       [sU[j].wtok]) for d in range(DC)])
                        sg = consume(bG, "scalar", lambda e, bG=bG: e.activation(
                            scr[:, 0:512], bG.t[:, 0:512], AF.Silu), waits=[prev_hbm])
                        hb_toks[(th, hc)] = consume(bU, "vector",
                            lambda e, bU=bU, hc=hc, th=th: e.tensor_tensor(
                                hb_dst[th][:, hc * 512:(hc + 1) * 512],
                                scr[:, 0:512], bU.t[:, 0:512], ALU.mult),
                            waits=[sg, hb_guard])
                        prev_hbm = hb_toks[(th, hc)]
                sG[j].rtok = penow()
                sU[j].rtok = penow()
                if j == 0:
                    load_slot(slots[0], WsG_src[:, :, 3 * 512:4 * 512])
                    load_slot(slots[1], WsU_src[:, :, 3 * 512:4 * 512])
                elif j == 1:
                    load_slot(slots[2], WsD_src[:, 0:4, :])
                    load_slot(slots[3], WsD_src[:, 4:8, :])
                elif j == 2:
                    load_slot(slots[4], WsD_src[:, 8:12, :])
                    load_slot(slots[5], WsD_src[:, 12:16, :])
            g_done = None
            for th in range(2):
                for dout in range(DC):
                    bk = banks[dout]
                    idx = 0
                    tok = None
                    for hc in range(16):
                        w = [sD[hc // 4].wtok, hb_toks[(th, hc)]]
                        if idx == 0:
                            w.extend(bk.rtok)

                        def f(e, bk=bk, sl=sD[hc // 4], hc=hc, th=th, dout=dout, idx=idx):
                            return e.matmul(
                                bk.t[:, 0:512],
                                sl.buf[:, (hc % 4) * 1024 + dout * 128:(hc % 4) * 1024 + (dout + 1) * 128],
                                hb_dst[th][:, hc * 512:(hc + 1) * 512],
                                start=(idx == 0), stop=(idx == 15))
                        tok = P.do("tensor", f, waits=w,
                                   inc=("pe" if idx == 15 else None))
                        idx += 1
                    bk.wtok = tok
                    v1 = consume(bk, "vector", lambda e, bk=bk, th=th: e.tensor_tensor(
                        scr[:, 0:512], bk.t[:, 0:512],
                        sigB[:, th * 512:(th + 1) * 512], ALU.mult),
                        waits=[sig_done])
                    v2 = P.do("vector", lambda e, dout=dout, th=th: e.tensor_tensor(
                        scr[:, 1024:1536], scr[:, 0:512],
                        y_v[:, dout, th * 512:(th + 1) * 512], ALU.add),
                        waits=[v1, y_done, g_done], inc="dve")
                    g_done = P.do("scalar", lambda e, dout=dout, th=th: e.activation(
                        y_v[:, dout, th * 512:(th + 1) * 512],
                        scr[:, 1024:1536], AF.Silu), waits=[v2], inc="act")
            for j in range(4):
                sD[j].rtok = penow()
            reg_dbg("g", ybuf[:, :], 128, DC * W, BF16)
            if maxphase < 6:
                bar = {"pe": penow(), "dve": ("dve", P.cnt["dve"]), "act": ("act", P.cnt["act"])}
                continue

            prev_ptr = None
            for tt in range(NTT):
                xr = syncdma(xin[tt % 2],
                             x_h[seq, row0 + tt * 128: row0 + (tt + 1) * 128, :],
                             waits=[last_x_use[tt % 2]])
                pv = P.do("vector", lambda e, tt=tt: e.tensor_tensor(
                    scr[:, 0:1024].rearrange("q (c w) -> q c w", w=128),
                    a_v[:, :, tt * 128:(tt + 1) * 128],
                    y_v[:, :, tt * 128:(tt + 1) * 128], ALU.mult),
                    waits=[g_done, s_done, prev_ptr], inc="dve")
                otoks = []
                for g in range(2):
                    bT = nextbank()
                    tr_group(bT, [(bT.t[:, j * 128:(j + 1) * 128],
                                   scr[:, (g * 4 + j) * 128:(g * 4 + j + 1) * 128],
                                   [pv, cs]) for j in range(4)])
                    prev_ptr = bT.wtok
                    otoks.append(consume(bT, "vector",
                        lambda e, bT=bT, tt=tt, g=g: e.tensor_tensor(
                            scr[:, 4096 + (tt % 2) * 1024 + g * 512:
                                4096 + (tt % 2) * 1024 + (g + 1) * 512],
                            bT.t[:, 0:512],
                            xin[tt % 2][:, g * 512:(g + 1) * 512], ALU.add),
                        waits=[xr, outst_free[tt % 2]]))
                stok = storedma(y_h[seq, row0 + tt * 128: row0 + (tt + 1) * 128, :],
                                scr[:, 4096 + (tt % 2) * 1024: 4096 + (tt % 2 + 1) * 1024],
                                waits=otoks)
                outst_free[tt % 2] = stok
                last_x_use[tt % 2] = otoks[-1]
            bar = {"pe": penow(), "dve": ("dve", P.cnt["dve"]), "act": ("act", P.cnt["act"])}

        waitall = [("pe", P.cnt["pe"]), ("dve", P.cnt["dve"]), ("act", P.cnt["act"]),
                   ("wl", P.cnt["wl"]), ("xl", P.cnt["xl"])]
        for name, (ap, out) in dbg_regs.items():
            P.do("gpsimd", lambda e, ap=ap, out=out: e.dma_start(out, ap),
                 waits=waitall, inc="so", amt=16)
        P.do("gpsimd", lambda e: e.nop(),
             waits=[("so", P.cnt["so"])] + waitall)
        P.finalize(stack)
    return nc


def _host_pack(inputs):
    f = np.float32
    asc = np.ascontiguousarray
    d = {}
    d["wq"] = asc(inputs["L0_attn_wq"], dtype=f)
    d["wk"] = asc(inputs["L0_attn_wk"], dtype=f)
    d["wv"] = asc(inputs["L0_attn_wv"], dtype=f)
    d["wo"] = asc(inputs["L0_attn_wo"], dtype=f)
    pj = np.zeros((128, 128), dtype=f)
    pj[0:64, 0:64] = np.asarray(inputs["L0_attn_proj"], dtype=f)
    pj[64:128, 64:128] = pj[0:64, 0:64]
    d["proj"] = pj
    d["w1"] = asc(inputs["L0_attn_w1"], dtype=f)
    d["w2"] = asc(inputs["L0_attn_w2"], dtype=f)
    d["b1c"] = asc(np.asarray(inputs["L0_attn_b1"], dtype=f).reshape(FF // 128, 128).T)
    d["b2c"] = asc(np.asarray(inputs["L0_attn_b2"], dtype=f).reshape(DC, 128).T)
    d["g1c"] = asc(np.asarray(inputs["L0_attn_g1"], dtype=f).reshape(DC, 128).T)
    d["be1c"] = asc(np.asarray(inputs["L0_attn_beta1"], dtype=f).reshape(DC, 128).T)
    d["g2c"] = asc(np.asarray(inputs["L0_attn_g2"], dtype=f).reshape(DC, 128).T)
    d["be2c"] = asc(np.asarray(inputs["L0_attn_beta2"], dtype=f).reshape(DC, 128).T)
    d["WgT"] = asc(np.asarray(inputs["L0_ffn_Wg"], dtype=f).T)
    d["WeGT"] = asc(np.asarray(inputs["L0_ffn_We_gate"], dtype=f).transpose(0, 2, 1))
    d["WeUT"] = asc(np.asarray(inputs["L0_ffn_We_up"], dtype=f).transpose(0, 2, 1))
    d["WeDT"] = asc(np.asarray(inputs["L0_ffn_We_down"], dtype=f).transpose(0, 2, 1))
    d["WsGT"] = asc(np.asarray(inputs["L0_ffn_Ws_gate"], dtype=f).T)
    d["WsUT"] = asc(np.asarray(inputs["L0_ffn_Ws_up"], dtype=f).T)
    d["WsDT"] = asc(np.asarray(inputs["L0_ffn_Ws_down"], dtype=f).T)
    d["WsgT"] = asc(np.asarray(inputs["L0_ffn_Wsg"], dtype=f).reshape(DC, 128).T)
    d["idf"] = np.eye(128, dtype=f)
    d["maskT"] = asc(np.tile(np.triu(np.ones((128, 128), dtype=f)), (1, 4)))
    d["onehot"] = asc(np.kron(np.eye(NE), np.ones((1, 128))).astype(ml_dtypes.bfloat16))
    d["onesf"] = np.ones((128, 128), dtype=f)
    d["onesb"] = np.ones((128, 128), dtype=ml_dtypes.bfloat16)
    return d


def kernel(**inputs):
    x = np.ascontiguousarray(np.asarray(inputs["L0_attn_x"], dtype=np.float32))
    shared = _host_pack(inputs)
    nc = bass.Bass("TRN2", target_bir_lowering=False)
    build(nc)
    in_maps = []
    for c in range(NCORE):
        m = dict(shared)
        m["x"] = np.ascontiguousarray(x[c * BPC:(c + 1) * BPC])
        in_maps.append(m)
    res = run_bass_kernel_spmd(nc, in_maps, core_ids=list(range(NCORE)))
    return np.concatenate([r["y"] for r in res.results], axis=0)


# version 07
# speedup vs baseline: 1.2118x; 1.0816x over previous
import numpy as np
import ml_dtypes
from contextlib import ExitStack

import concourse.bass as bass
import concourse.mybir as mybir
from concourse.bass_utils import run_bass_kernel_spmd

F32 = mybir.dt.float32
BF16 = mybir.dt.bfloat16
AF = mybir.ActivationFunctionType
ALU = mybir.AluOpType
AX = mybir.AxisListType

B, S, D = 32, 2048, 1024
NCORE = 8
BPC = B // NCORE
W = 1024
NPASS = BPC * (S // W)
NTT = W // 128
DC = D // 128
H, DH = 16, 64
FF = 4096
NE, HID = 16, 512
SH = 2048
EPS = 1e-5

ENGS = ("sync", "tensor", "vector", "scalar", "gpsimd")


class Prog:
    def __init__(self, nc, stack):
        self.nc = nc
        self.q = {e: [] for e in ENGS}
        self.sems = {}
        self.cnt = {}
        self.lastw = {e: {} for e in ENGS}
        for s in ("pe", "dve", "act", "wl", "xl", "so"):
            self.sems[s] = stack.enter_context(nc.semaphore(s))
            self.cnt[s] = 0

    def do(self, eng, fn, waits=(), inc=None, amt=1):
        ws = []
        for t in waits:
            if t is None:
                continue
            s, v = t
            assert v <= self.cnt[s], f"wait on future {s}:{v}>{self.cnt[s]}"
            if self.lastw[eng].get(s, 0) < v:
                self.lastw[eng][s] = v
                ws.append((s, v))
        tok = None
        if inc is not None:
            self.cnt[inc] += amt
            tok = (inc, self.cnt[inc])
        self.q[eng].append((ws, fn, tok, amt))
        return tok

    def finalize(self, stack):
        block = stack.enter_context(self.nc.Block())
        for eng in ENGS:
            ops = self.q[eng]
            if not ops:
                continue

            def body(e, ops=ops):
                for ws, fn, tok, amt in ops:
                    for (s, v) in ws:
                        e.wait_ge(self.sems[s], v)
                    ins = fn(e)
                    if tok is not None:
                        ins.then_inc(self.sems[tok[0]], amt)
            getattr(block, eng)(body)


class Bank:
    def __init__(self, t):
        self.t = t
        self.wtok = None
        self.rtok = []


class Slot:
    def __init__(self, buf):
        self.buf = buf
        self.wtok = None
        self.rtok = None


def build(nc, npass=NPASS, maxphase=6, dbg=(), sub=9, p2cut=9):
    dt = nc.dram_tensor
    x_h = dt("x", [BPC, S, D], F32, kind="ExternalInput").ap()
    ins = {}
    for nm, shp in (("wq", [D, D]), ("wk", [D, D]), ("wv", [D, D]), ("wo", [D, D]),
                    ("proj", [128, 128]), ("w1", [D, FF]), ("w2", [FF, D]),
                    ("b1c", [128, FF // 128]), ("b2c", [128, DC]),
                    ("g1c", [128, DC]), ("be1c", [128, DC]),
                    ("g2c", [128, DC]), ("be2c", [128, DC]),
                    ("WgT", [D, NE]), ("WeGT", [NE, D, HID]), ("WeUT", [NE, D, HID]),
                    ("WeDT", [NE, HID, D]), ("WsGT", [D, SH]), ("WsUT", [D, SH]),
                    ("WsDT", [SH, D]), ("WsgT", [128, DC]),
                    ("idf", [128, 128]), ("maskT", [128, 512]),
                    ("onesf", [128, 128]), ("sel8", [8, 512])):
        ins[nm] = dt(nm, shp, F32, kind="ExternalInput").ap()
    ins["onehot"] = dt("onehot", [NE, NE * 128], BF16, kind="ExternalInput").ap()
    onesb_h = dt("onesb", [128, 128], BF16, kind="ExternalInput").ap()
    y_h = dt("y", [BPC, S, D], F32, kind="ExternalOutput").ap()
    wcache_h = dt("wcache", [128, 128, 4096], BF16, kind="Internal").ap()

    stack = ExitStack()
    with stack:
        sb = lambda n, shp, d=BF16: stack.enter_context(nc.sbuf_tensor(n, shp, d))
        idf = sb("s_idf", [128, 128], F32)
        maskT = sb("s_maskT", [128, 512], F32)
        onehot = sb("s_onehot", [NE, NE * 128], BF16)
        onesb = sb("s_onesb", [128, 128], BF16)
        onesf = sb("s_onesf", [128, 128], F32)
        sel8 = sb("s_sel8", [8, 512], F32)
        proj = sb("projb", [128, 128], BF16)
        WgTs = sb("WgTs", [128, DC * NE], BF16)
        WsgTs = sb("WsgTs", [128, DC], BF16)
        b1c = sb("s_b1c", [128, FF // 128], F32)
        b2c = sb("s_b2c", [128, DC], F32)
        g1c = sb("s_g1c", [128, DC], F32)
        be1c = sb("s_be1c", [128, DC], F32)
        g2c = sb("s_g2c", [128, DC], F32)
        be2c = sb("s_be2c", [128, DC], F32)
        kvf = sb("kvf", [64, H * DH], F32)
        ksf = sb("ksf", [64, H], F32)
        kvb = sb("kvb", [128, H * DH], BF16)
        ksb = sb("ksb", [128, H], BF16)
        xn = sb("xn", [128, DC * W])
        hbuf = sb("hbuf", [128, DC * W])
        abuf = sb("abuf", [128, DC * W])
        ybuf = sb("ybuf", [128, DC * W])
        big4 = sb("big4", [128, 4 * DC * 512])
        qraw = sb("qraw", [128, DC * 512])
        sigB = sb("sigB", [128, W], BF16)
        wT = sb("wT", [NE, W], BF16)
        w_tm = sb("w_tm", [128, NTT * NE], F32)
        smallb = sb("smallb", [128, 2048], BF16)
        stats = sb("stats", [128, 64], F32)
        rbt = sb("rbt", [64, 1024], F32)
        scr = sb("scr", [128, 6208], F32)
        wslot = [sb(f"ws{i}", [128, 4096]) for i in range(6)]

        ps = [stack.enter_context(nc.psum_tensor(f"p{i}", [128, 512], F32))
              for i in range(8)]
        P = Prog(nc, stack)
        banks = [Bank(t) for t in ps]
        slots = [Slot(t) for t in wslot]

        xn_v = xn[:, :].rearrange("q (c w) -> q c w", w=W)
        h_v = hbuf[:, :].rearrange("q (c w) -> q c w", w=W)
        a_v = abuf[:, :].rearrange("q (c w) -> q c w", w=W)
        y_v = ybuf[:, :].rearrange("q (c w) -> q c w", w=W)

        def syncdma(dst, src, waits=()):
            return P.do("sync", lambda e, d=dst, s=src: e.dma_start(d, s),
                        waits=waits, inc="xl", amt=16)

        def storedma(dst, src, waits=()):
            return P.do("sync", lambda e, d=dst, s=src: e.dma_start(d, s),
                        waits=waits, inc="so", amt=16)

        def gpdma(dst, src, waits=()):
            return P.do("gpsimd", lambda e, d=dst, s=src: e.dma_start(d, s),
                        waits=waits, inc="wl", amt=16)

        wload = {"i": 0, "pass0": True}

        def load_slot(sl, src):
            if wload["pass0"]:
                sl.wtok = gpdma(sl.buf[:, :], src, waits=[sl.rtok, getattr(sl, "wb", None)])
                sl.wb = gpdma(wcache_h[wload["i"]], sl.buf[:, :], waits=[sl.wtok])
            else:
                sl.wtok = P.do("sync", lambda e, d=sl.buf[:, :], s=wcache_h[wload["i"]]:
                               e.dma_start(d, s),
                               waits=[sl.rtok, getattr(sl, "wb", None)], inc="xl", amt=16)
                sl.wb = None
            wload["i"] += 1
            sl.rtok = None
            return sl.wtok

        rot = {"b": 0}

        def nextbank():
            i = rot["b"]
            rot["b"] = (i + 1) % 6
            return banks[i]

        def mm_group(bank, mms):
            n = len(mms)
            tok = None
            for i, item in enumerate(mms):
                (out, l, r, ws), tp = (item[:4], item[4] if len(item) > 4 else None)
                w = list(ws)
                if i == 0:
                    w.extend(bank.rtok)

                def f(e, out=out, l=l, r=r, i=i, n=n, tp=tp):
                    return e.matmul(out, l, r, start=(i == 0), stop=(i == n - 1),
                                    tile_position=tp)
                tok = P.do("tensor", f, waits=w,
                           inc=("pe" if i == n - 1 else None))
            bank.wtok = tok
            return tok

        def tr_group(bank, items):
            tok = None
            for i, (out, inp, ws) in enumerate(items):
                w = list(ws)
                if i == 0:
                    w.extend(bank.rtok)

                def f(e, out=out, inp=inp):
                    return e.transpose(out, inp, idf[:, :])
                tok = P.do("tensor", f, waits=w,
                           inc=("pe" if i == len(items) - 1 else None))
            bank.wtok = tok
            return tok

        def consume(bank, eng, fn, waits=(), sem=None):
            sem = sem or ("dve" if eng == "vector" else "act")
            tok = P.do(eng, fn, waits=[bank.wtok] + list(waits), inc=sem)
            bank.rtok = [tok]
            return tok

        def penow():
            return ("pe", P.cnt["pe"])

        ctoks = [
            syncdma(idf[:, :], ins["idf"]),
            syncdma(maskT[:, :], ins["maskT"]),
            syncdma(onehot[:, :], ins["onehot"]),
            syncdma(onesb[:, :], onesb_h),
            syncdma(onesf[:, :], ins["onesf"]),
            syncdma(sel8[:, :], ins["sel8"]),
            syncdma(b1c[:, :], ins["b1c"]),
            syncdma(b2c[:, :], ins["b2c"]),
            syncdma(g1c[:, :], ins["g1c"]),
            syncdma(be1c[:, :], ins["be1c"]),
            syncdma(g2c[:, :], ins["g2c"]),
            syncdma(be2c[:, :], ins["be2c"]),
            gpdma(proj[:, :], ins["proj"]),
            gpdma(WgTs[:, :], ins["WgT"].rearrange("(c p) m -> p c m", p=128)),
            gpdma(WsgTs[:, :], ins["WsgT"]),
        ]
        ck = ctoks[-1]
        cs = ctoks[-4]
        t_eps = P.do("vector", lambda e: e.memset(stats[:, 63:64], EPS), inc="dve")
        t_eps2 = P.do("vector", lambda e: e.memset(stats[:, 62:63], 1e-6), inc="dve")

        wq_src = ins["wq"].rearrange("(c p) m -> p c m", p=128)
        wk_src = ins["wk"].rearrange("(c p) m -> p c m", p=128)
        wv_src = ins["wv"].rearrange("(c p) m -> p c m", p=128)
        wo_src = ins["wo"].rearrange("(c p) m -> p c m", p=128)
        w1_src = ins["w1"].rearrange("(c p) m -> p c m", p=128)
        w2_src = ins["w2"].rearrange("(c p) m -> p c m", p=128)
        WeG_src = [ins["WeGT"][e].rearrange("(c p) m -> p c m", p=128) for e in range(NE)]
        WeU_src = [ins["WeUT"][e].rearrange("(c p) m -> p c m", p=128) for e in range(NE)]
        WeD_src = [ins["WeDT"][e].rearrange("(c p) m -> p c m", p=128) for e in range(NE)]
        WsG_src = ins["WsGT"].rearrange("(c p) m -> p c m", p=128)
        WsU_src = ins["WsUT"].rearrange("(c p) m -> p c m", p=128)
        WsD_src = ins["WsDT"].rearrange("(c p) m -> p c m", p=128)

        dbgset = set(dbg)
        dbg_regs = {}

        def reg_dbg(name, ap, pdim, fdim, dtype):
            if name in dbgset and name not in dbg_regs:
                o = dt("dbg_" + name, [pdim, fdim], dtype, kind="ExternalOutput").ap()
                dbg_regs[name] = (ap, o)

        last_x_use = [None, None]
        outst_free = [None, None]
        xre_free = [None, None]
        state_tok = {0: None, 1: None}
        bar = {"pe": None, "dve": None, "act": None}

        for p in range(npass):
            seq, half = p // 2, p % 2
            row0 = half * W
            wload["i"] = 0
            wload["pass0"] = (p == 0)
            pbar = [bar["pe"], bar["dve"], bar["act"]]

            if half == 0:
                P.do("vector", lambda e: e.memset(kvf[:, :], 0.0), waits=pbar, inc="dve")
                P.do("vector", lambda e: e.memset(ksf[:, :], 0.0), inc="dve")
                P.do("vector", lambda e: e.memset(kvb[:, :], 0.0), inc="dve")
                z4 = P.do("vector", lambda e: e.memset(ksb[:, :], 0.0), inc="dve")
                state_tok[0] = z4
                state_tok[1] = z4
            xin = [scr[:, 2048:3072], scr[:, 3072:4096]]
            xload = [None] * NTT

            xt_tok = [None] * NTT
            ht_tok = [None] * NTT
            prev_trA = None
            prev_trB = None

            def emit_p1_tile(tt):
                nonlocal prev_trA, prev_trB
                xload[tt] = syncdma(xin[tt % 2],
                                    x_h[seq, row0 + tt * 128: row0 + (tt + 1) * 128, :],
                                    waits=[last_x_use[tt % 2]])
                xt = xin[tt % 2]
                P.do("vector", lambda e, xt=xt: e.bn_stats(
                    stats[:, 16:22], xt[:, 0:512]),
                    waits=[xload[tt]] + pbar, inc="dve")
                P.do("vector", lambda e, xt=xt: e.bn_stats(
                    stats[:, 22:28], xt[:, 512:1024]), inc="dve")
                t4b = P.do("scalar", lambda e: e.copy(
                    stats[:, 40:52], stats[:, 16:28]),
                    waits=[("dve", P.cnt["dve"])], inc="act")
                t5 = P.do("vector", lambda e: e.bn_aggr(
                    stats[:, 0:2], stats[:, 40:52]), waits=[t4b], inc="dve")
                t5b = P.do("scalar", lambda e: e.copy(
                    stats[:, 2:4], stats[:, 0:2]), waits=[t5], inc="act")
                t6 = P.do("scalar", lambda e: e.activation(
                    stats[:, 5:6], stats[:, 3:4], AF.Ln, bias=stats[:, 63:64]),
                    waits=[t5b, t_eps], inc="act")
                t7 = P.do("scalar", lambda e: e.activation(
                    stats[:, 6:7], stats[:, 5:6], AF.Exp, scale=-0.5),
                    waits=[t6], inc="act")
                t8 = P.do("vector", lambda e, xt=xt: e.tensor_scalar(
                    scr[:, 0:1024], xt, stats[:, 2:3], stats[:, 6:7],
                    ALU.subtract, ALU.mult), waits=[t7, prev_trA], inc="dve")
                P.do("vector", lambda e: e.bn_stats(
                    stats[:, 16:22], scr[:, 0:512]), waits=[t8], inc="dve")
                P.do("vector", lambda e: e.bn_stats(
                    stats[:, 22:28], scr[:, 512:1024]), inc="dve")
                u4b = P.do("scalar", lambda e: e.copy(
                    stats[:, 40:52], stats[:, 16:28]),
                    waits=[("dve", P.cnt["dve"])], inc="act")
                u5 = P.do("vector", lambda e: e.bn_aggr(
                    stats[:, 8:10], stats[:, 40:52]), waits=[u4b], inc="dve")
                u5b = P.do("scalar", lambda e: e.copy(
                    stats[:, 10:12], stats[:, 8:10]), waits=[u5], inc="act")
                u6 = P.do("scalar", lambda e: e.activation(
                    stats[:, 12:13], stats[:, 11:12], AF.Ln, bias=stats[:, 63:64]),
                    waits=[u5b], inc="act")
                u7 = P.do("scalar", lambda e: e.activation(
                    stats[:, 13:14], stats[:, 12:13], AF.Exp, scale=-0.5),
                    waits=[u6], inc="act")
                u8 = P.do("vector", lambda e: e.tensor_scalar(
                    scr[:, 1024:2048], scr[:, 0:1024], stats[:, 10:11], stats[:, 13:14],
                    ALU.subtract, ALU.mult), waits=[u7, prev_trB], inc="dve")
                last_x_use[tt % 2] = t8
                for g in range(2):
                    bA = nextbank()
                    tr_group(bA, [(bA.t[:, j * 128:(j + 1) * 128],
                                   scr[:, (g * 4 + j) * 128:(g * 4 + j + 1) * 128],
                                   [t8, cs]) for j in range(4)])
                    prev_trA = bA.wtok
                    tokx = consume(bA, "scalar", lambda e, bA=bA, g=g, tt=tt: e.copy(
                        xn_v[:, g * 4:(g + 1) * 4, tt * 128:(tt + 1) * 128],
                        bA.t[:, 0:512].rearrange("q (c w) -> q c w", w=128)))
                    bB = nextbank()
                    tr_group(bB, [(bB.t[:, j * 128:(j + 1) * 128],
                                   scr[:, 1024 + (g * 4 + j) * 128:1024 + (g * 4 + j + 1) * 128],
                                   [u8]) for j in range(4)])
                    prev_trB = bB.wtok
                    tokh = None
                    for j in range(4):
                        d = g * 4 + j
                        tokh = P.do("vector", lambda e, bB=bB, j=j, d=d, tt=tt: e.tensor_scalar(
                            hbuf[:, d * W + tt * 128: d * W + (tt + 1) * 128],
                            bB.t[:, j * 128:(j + 1) * 128],
                            g1c[:, d:d + 1], be1c[:, d:d + 1], ALU.mult, ALU.add),
                            waits=[bB.wtok, cs], inc="dve")
                    bB.rtok = [tokh]
                    if g == 1:
                        xt_tok[tt] = tokx
                        ht_tok[tt] = tokh

            q_of, k_of, v_of, kt_of = 0, DC * 512, 2 * DC * 512, 3 * DC * 512
            atw = {}

            def emit_attn_loads():
                for i, (nm, src_) in enumerate((("wq", wq_src), ("wk", wk_src), ("wv", wv_src))):
                    for hf in range(2):
                        sl = slots[i * 2 + hf]
                        load_slot(sl, src_[:, :, hf * 512:(hf + 1) * 512])
                        atw[(nm, hf)] = sl

            qkv_free = None
            smr = [None, None]
            drr = None
            rbr = [None, None]
            attn_last = [None, None]
            scan_i = 0
            pst = {}

            def emit_projs(st):
                nonlocal qkv_free
                phi_q_done = None
                phi_last = {}
                rev_k = None
                prev_elu = None
                projs = () if p2cut == 0 else ((("wq", q_of),) if p2cut in (1, 11, 12, 13, 14) else (("wq", q_of), ("wk", k_of)))
                for nm, dstof in projs:
                    rev_toks = []
                    for c in range(DC):
                        bR = nextbank()
                        mm_group(bR, [(bR.t[:, 0:512],
                                       atw[(nm, c // 4)].buf[:, d * 512 + (c % 4) * 128: d * 512 + (c % 4) * 128 + 128],
                                       h_v[:, d, st * 512:(st + 1) * 512],
                                       [ht_tok[st * 4 + 3], atw[(nm, c // 4)].wtok])
                                      for d in range(DC)])
                        rev = consume(bR, "vector", lambda e, bR=bR, c=c: e.tensor_copy(
                            qraw[:, c * 512:(c + 1) * 512], bR.t[:, 0:512]),
                            waits=[qkv_free, phi_q_done])
                        rev_toks.append(rev)
                        if p2cut == 11:
                            phi_last[nm] = rev
                            continue
                        bP = nextbank()
                        mm_group(bP, [(bP.t[:, 0:512], proj[:, :],
                                       qraw[:, c * 512:(c + 1) * 512], [rev, ck])])
                        if p2cut == 13:
                            ev0 = consume(bP, "vector", lambda e, bP=bP, dstof=dstof, c=c:
                                e.tensor_copy(big4[:, dstof + c * 512: dstof + (c + 1) * 512],
                                              bP.t[:, 0:512]), waits=[qkv_free])
                            prev_elu = ev0
                            phi_last[nm] = ev0
                            continue
                        e1 = P.do("vector", lambda e, bP=bP: e.tensor_scalar(
                            scr[:, 0:512], bP.t[:, 0:512], 0.0, None, ALU.max),
                            waits=[bP.wtok, prev_elu], inc="dve")
                        e2 = P.do("vector", lambda e, bP=bP: e.tensor_scalar(
                            scr[:, 1024:1536], bP.t[:, 0:512], 0.0, None, ALU.min),
                            waits=[bP.wtok], inc="dve")
                        bP.rtok = [e2]
                        e3 = P.do("scalar", lambda e: e.activation(
                            scr[:, 1024:1536], scr[:, 1024:1536], AF.Exp),
                            waits=[e2], inc="act")
                        e4 = P.do("vector", lambda e, dstof=dstof, c=c: e.tensor_tensor(
                            big4[:, dstof + c * 512: dstof + (c + 1) * 512],
                            scr[:, 0:512], scr[:, 1024:1536], ALU.add),
                            waits=[e1, e3, qkv_free], inc="dve")
                        prev_elu = e4
                        phi_last[nm] = e4
                    if nm == "wq":
                        phi_q_done = penow()
                    else:
                        rev_k = rev_toks
                ktm_toks = []
                for tc in range(4 if 3 <= p2cut <= 9 else 0):
                    for hg in range(2):
                        bK = nextbank()
                        mms = []
                        for pp in range(4):
                            pair = hg * 4 + pp
                            mms.append((bK.t[:, pp * 128:(pp + 1) * 128],
                                        qraw[:, pair * 512 + tc * 128: pair * 512 + (tc + 1) * 128],
                                        proj[:, :],
                                        [rev_k[pair], ck]))
                        mm_group(bK, mms)
                        f1 = P.do("vector", lambda e, bK=bK: e.tensor_scalar(
                            scr[:, 0:512], bK.t[:, 0:512], 0.0, None, ALU.max),
                            waits=[bK.wtok, prev_elu], inc="dve")
                        f2 = P.do("vector", lambda e, bK=bK: e.tensor_scalar(
                            scr[:, 1024:1536], bK.t[:, 0:512], 0.0, None, ALU.min),
                            waits=[bK.wtok], inc="dve")
                        bK.rtok = [f2]
                        f3 = P.do("scalar", lambda e: e.activation(
                            scr[:, 1024:1536], scr[:, 1024:1536], AF.Exp),
                            waits=[f2], inc="act")
                        f4 = P.do("vector", lambda e, tc=tc, hg=hg: e.tensor_tensor(
                            big4[:, kt_of + tc * 1024 + hg * 512: kt_of + tc * 1024 + (hg + 1) * 512],
                            scr[:, 0:512], scr[:, 1024:1536], ALU.add),
                            waits=[f1, f3, qkv_free], inc="dve")
                        prev_elu = f4
                        ktm_toks.append(f4)
                v_toks = {}
                for tc in range(4 if 4 <= p2cut <= 9 else 0):
                    for cf in range(2):
                        bV = nextbank()
                        mm_group(bV, [(bV.t[:, 0:512],
                                       h_v[:, d, st * 512 + tc * 128: st * 512 + (tc + 1) * 128],
                                       atw[("wv", cf)].buf[:, d * 512:(d + 1) * 512],
                                       [ht_tok[st * 4 + tc], atw[("wv", cf)].wtok])
                                      for d in range(DC)])
                        v_toks[(tc, cf)] = consume(bV, "scalar",
                            lambda e, bV=bV, tc=tc, cf=cf: e.copy(
                                big4[:, v_of + tc * 1024 + cf * 512: v_of + tc * 1024 + (cf + 1) * 512],
                                bV.t[:, 0:512]), waits=[qkv_free])
                if st == 1 and p2cut >= 1:
                    for key in atw:
                        atw[key].rtok = penow()
                pst[st] = (phi_last, rev_k, ktm_toks, v_toks)

            def emit_scan(st):
                nonlocal qkv_free, drr, scan_i, rbr
                phi_last, rev_k, ktm_toks, v_toks = pst[st]
                if sub < 2:
                    qkv_free = penow()
                    attn_last[st] = ("dve", P.cnt["dve"])
                    return
                for tc in range(4):
                    for p_ in range(2):
                        i2 = scan_i % 2
                        base = p_ * 64
                        S_mw = smallb[:, i2 * 1024:(i2 + 1) * 1024]
                        qfs = [big4[base:base + 64,
                                    q_of + i * 512 + tc * 128: q_of + i * 512 + (tc + 1) * 128]
                               for i in range(8)]
                        kfs = [big4[base:base + 64,
                                    k_of + i * 512 + tc * 128: k_of + i * 512 + (tc + 1) * 128]
                               for i in range(8)]
                        vts = [big4[:, v_of + tc * 1024 + (p_ + 2 * i) * 64:
                                    v_of + tc * 1024 + (p_ + 2 * i + 1) * 64] for i in range(8)]
                        kts = [big4[:, kt_of + tc * 1024 + (p_ + 2 * i) * 64:
                                    kt_of + tc * 1024 + (p_ + 2 * i + 1) * 64] for i in range(8)]
                        bS0 = nextbank(); bS1 = nextbank()
                        mm_group(bS0, [(bS0.t[:, (i % 4) * 128:(i % 4 + 1) * 128],
                                        kfs[i], qfs[i],
                                        [phi_last["wk"], phi_last["wq"]]) for i in range(4)])
                        mm_group(bS1, [(bS1.t[:, (i % 4) * 128:(i % 4 + 1) * 128],
                                        kfs[i + 4], qfs[i + 4], []) for i in range(4)])
                        mk0 = consume(bS0, "vector", lambda e, bS0=bS0, S_mw=S_mw: e.tensor_tensor(
                            S_mw[:, 0:512], bS0.t[:, 0:512], maskT[:, :], ALU.mult),
                            waits=[smr[i2], cs])
                        mk1 = consume(bS1, "vector", lambda e, bS1=bS1, S_mw=S_mw: e.tensor_tensor(
                            S_mw[:, 512:1024], bS1.t[:, 0:512], maskT[:, :], ALU.mult))
                        bN0 = nextbank(); bN1 = nextbank()
                        mms0, mms1 = [], []
                        for i in range(8):
                            bb, mm, mkt = (bN0, mms0, mk0) if i < 4 else (bN1, mms1, mk1)
                            reg = bb.t[0:64, (i % 4) * 128:(i % 4 + 1) * 128]
                            mm.append((reg, vts[i],
                                       S_mw[:, i * 128:(i + 1) * 128],
                                       [mkt, v_toks[(tc, i // 4)]]))
                            mm.append((reg,
                                       kvb[base:base + 64, p_ * 512 + i * 64: p_ * 512 + (i + 1) * 64],
                                       qfs[i], [state_tok[p_]]))
                        mm_group(bN0, mms0)
                        mm_group(bN1, mms1)
                        bDK = nextbank()
                        mmd = []
                        for i in range(8):
                            reg = bDK.t[:, i:i + 1]
                            mmd.append((reg, S_mw[:, i * 128:(i + 1) * 128],
                                        onesb[:, 0:1], []))
                            mmd.append((reg, qfs[i],
                                        ksb[base:base + 64, p_ * 8 + i: p_ * 8 + i + 1],
                                        [state_tok[p_]]))
                        mm_group(bDK, mmd)
                        dd2 = consume(bDK, "scalar", lambda e, bDK=bDK: e.activation(
                            scr[:, 6152:6160], bDK.t[:, 0:8], AF.Ln, bias=stats[:, 62:63]),
                            waits=[drr, t_eps2, gde])
                        dd3 = P.do("scalar", lambda e: e.activation(
                            scr[:, 6152:6160], scr[:, 6152:6160], AF.Exp, scale=-1.0),
                            waits=[dd2], inc="act")
                        def trf(e, bDK=bDK):
                            return e.transpose(bDK.t[0:8, 8:136], scr[:, 6152:6160], idf[:, :])
                        trtok = P.do("tensor", trf, waits=[dd3, cs], inc="pe")
                        dd4 = P.do("scalar", lambda e, bDK=bDK: e.copy(
                            scr[0:8, 5120:5248], bDK.t[0:8, 8:136]),
                            waits=[trtok, drr], inc="act")
                        bDK.wtok = trtok
                        bDK.rtok = [dd4]
                        mm_group(bS0, [(bS0.t[0:64, (i % 4) * 128:(i % 4 + 1) * 128],
                                        sel8[:, i * 64:(i + 1) * 64],
                                        scr[0:8, 5120:5248],
                                        [dd4] if i == 0 else []) for i in range(4)])
                        mm_group(bS1, [(bS1.t[0:64, (i % 4) * 128:(i % 4 + 1) * 128],
                                        sel8[:, (i + 4) * 64:(i + 5) * 64],
                                        scr[0:8, 5120:5248], []) for i in range(4)])
                        drr = bS1.wtok
                        rb0 = consume(bS0, "scalar", lambda e, bS0=bS0: e.copy(
                            rbt[0:64, 0:512], bS0.t[0:64, 0:512]), waits=[rbr[0]])
                        rb1 = consume(bS1, "scalar", lambda e, bS1=bS1: e.copy(
                            rbt[0:64, 512:1024], bS1.t[0:64, 0:512]), waits=[rbr[1]])
                        a0 = P.do("vector", lambda e, bN0=bN0, base=base, tc=tc, st=st:
                            e.tensor_tensor(
                                a_v[base:base + 64, 0:4,
                                    st * 512 + tc * 128: st * 512 + (tc + 1) * 128],
                                bN0.t[0:64, 0:512].rearrange("q (c w) -> q c w", w=128),
                                rbt[0:64, 0:512].rearrange("q (c w) -> q c w", w=128),
                                ALU.mult),
                            waits=[bN0.wtok, rb0], inc="dve")
                        a1 = P.do("vector", lambda e, bN1=bN1, base=base, tc=tc, st=st:
                            e.tensor_tensor(
                                a_v[base:base + 64, 4:8,
                                    st * 512 + tc * 128: st * 512 + (tc + 1) * 128],
                                bN1.t[0:64, 0:512].rearrange("q (c w) -> q c w", w=128),
                                rbt[0:64, 512:1024].rearrange("q (c w) -> q c w", w=128),
                                ALU.mult),
                            waits=[bN1.wtok, rb1], inc="dve")
                        bN0.rtok = [a0]
                        bN1.rtok = [a1]
                        rbr = [a0, a1]
                        attn_last[st] = a1
                        smr[i2] = penow()
                        bKV, bKS = banks[6], banks[7]
                        mm_group(bKV, [(bKV.t[0:64, i * 64:(i + 1) * 64], kts[i], vts[i],
                                        [ktm_toks[tc * 2 + i // 4], v_toks[(tc, i // 4)]])
                                       for i in range(8)])
                        mm_group(bKS, [(bKS.t[0:64, i:i + 1], kts[i], onesb[:, 0:1], [])
                                       for i in range(8)])
                        kva = consume(bKV, "vector", lambda e, bKV=bKV, p_=p_: e.tensor_tensor(
                            kvf[:, p_ * 512:(p_ + 1) * 512], kvf[:, p_ * 512:(p_ + 1) * 512],
                            bKV.t[0:64, 0:512], ALU.add))
                        ksa = consume(bKS, "vector", lambda e, bKS=bKS, p_=p_: e.tensor_tensor(
                            ksf[:, p_ * 8:(p_ + 1) * 8], ksf[:, p_ * 8:(p_ + 1) * 8],
                            bKS.t[0:64, 0:8], ALU.add))
                        P.do("scalar", lambda e, base=base, p_=p_: e.copy(
                            kvb[base:base + 64, p_ * 512:(p_ + 1) * 512],
                            kvf[:, p_ * 512:(p_ + 1) * 512]), waits=[kva], inc="act")
                        c2 = P.do("scalar", lambda e, base=base, p_=p_: e.copy(
                            ksb[base:base + 64, p_ * 8:(p_ + 1) * 8],
                            ksf[:, p_ * 8:(p_ + 1) * 8]), waits=[ksa], inc="act")
                        state_tok[p_] = c2
                        scan_i += 1
                qkv_free = penow()

            for tt in range(4):
                emit_p1_tile(tt)
            if maxphase >= 2:
                emit_attn_loads()
                emit_projs(0)
            for tt in range(4, 8):
                emit_p1_tile(tt)

            aF_toks = []
            for tt in range(NTT):
                bG = nextbank()
                mm_group(bG, [(bG.t[:, 0:NE],
                               xn[:, d * W + tt * 128: d * W + (tt + 1) * 128],
                               WgTs[:, d * NE:(d + 1) * NE],
                               [xt_tok[tt], ck]) for d in range(DC)])
                gsc = scr[:, 6144:6160]
                m1s = scr[:, 6160:6176]
                g2s = scr[:, 6176:6192]
                m2s = scr[:, 6192:6208]
                a1 = consume(bG, "vector", lambda e, bG=bG: e.tensor_reduce(
                    stats[:, 32:33], bG.t[:, 0:NE], AX.X, ALU.max))
                a1b = P.do("scalar", lambda e: e.copy(stats[:, 40:41], stats[:, 32:33]),
                           waits=[a1], inc="act")
                a2 = P.do("vector", lambda e, bG=bG, gsc=gsc: e.tensor_scalar(
                    gsc, bG.t[:, 0:NE], stats[:, 40:41], None, ALU.subtract),
                    waits=[a1b], inc="dve")
                bG.rtok = [a2]
                a3 = P.do("scalar", lambda e, gsc=gsc: e.activation(gsc, gsc, AF.Exp),
                          waits=[a2], inc="act")
                a4 = P.do("vector", lambda e, gsc=gsc: e.tensor_reduce(
                    stats[:, 33:34], gsc, AX.X, ALU.add), waits=[a3], inc="dve")
                a4b = P.do("scalar", lambda e: e.copy(stats[:, 41:42], stats[:, 33:34]),
                           waits=[a4], inc="act")
                a5 = P.do("vector", lambda e: e.reciprocal(stats[:, 34:35], stats[:, 41:42]),
                          waits=[a4b], inc="dve")
                a5b = P.do("scalar", lambda e: e.copy(stats[:, 42:43], stats[:, 34:35]),
                           waits=[a5], inc="act")
                a6 = P.do("vector", lambda e, gsc=gsc: e.tensor_scalar(
                    gsc, gsc, stats[:, 42:43], None, ALU.mult), waits=[a5b], inc="dve")
                a7 = P.do("vector", lambda e, gsc=gsc: e.tensor_reduce(
                    stats[:, 35:36], gsc, AX.X, ALU.max), waits=[a6], inc="dve")
                a7b = P.do("scalar", lambda e: e.copy(stats[:, 43:44], stats[:, 35:36]),
                           waits=[a7], inc="act")
                a8 = P.do("vector", lambda e, gsc=gsc, m1s=m1s: e.tensor_scalar(
                    m1s, gsc, stats[:, 43:44], None, ALU.is_ge), waits=[a7b], inc="dve")
                a9 = P.do("vector", lambda e, gsc=gsc, m1s=m1s, g2s=g2s: e.scalar_tensor_tensor(
                    g2s, m1s, -2.0, gsc, ALU.mult, ALU.add), waits=[a8], inc="dve")
                aA = P.do("vector", lambda e, g2s=g2s: e.tensor_reduce(
                    stats[:, 36:37], g2s, AX.X, ALU.max), waits=[a9], inc="dve")
                aAb = P.do("scalar", lambda e: e.copy(stats[:, 44:45], stats[:, 36:37]),
                           waits=[aA], inc="act")
                aB = P.do("vector", lambda e, g2s=g2s, m2s=m2s: e.tensor_scalar(
                    m2s, g2s, stats[:, 44:45], None, ALU.is_ge), waits=[aAb], inc="dve")
                aC = P.do("vector", lambda e: e.tensor_tensor(
                    stats[:, 37:38], stats[:, 43:44], stats[:, 44:45], ALU.add),
                    waits=[aB], inc="dve")
                aD = P.do("vector", lambda e: e.reciprocal(stats[:, 38:39], stats[:, 37:38]),
                          waits=[aC], inc="dve")
                aDb = P.do("scalar", lambda e: e.copy(stats[:, 45:46], stats[:, 38:39]),
                           waits=[aD], inc="act")
                aE = P.do("vector", lambda e, m1s=m1s, m2s=m2s: e.tensor_tensor(
                    m1s, m1s, m2s, ALU.add), waits=[aB], inc="dve")
                aF = P.do("vector", lambda e, gsc=gsc, m1s=m1s, tt=tt: e.scalar_tensor_tensor(
                    w_tm[:, tt * NE:(tt + 1) * NE], gsc, stats[:, 45:46], m1s,
                    ALU.mult, ALU.mult), waits=[aDb, aE], inc="dve")
                aF_toks.append(aF)
            for tt in range(NTT):
                bT = nextbank()
                tr_group(bT, [(bT.t[0:NE, 0:128], w_tm[:, tt * NE:(tt + 1) * NE],
                               [aF_toks[tt], cs])])
                consume(bT, "scalar", lambda e, bT=bT, tt=tt: e.copy(
                    wT[0:NE, tt * 128:(tt + 1) * 128], bT.t[0:NE, 0:128]))
            wT_tok = ("act", P.cnt["act"])

            for th in range(2):
                bS = nextbank()
                mm_group(bS, [(bS.t[0:1, 0:512], WsgTs[:, d:d + 1],
                               xn_v[:, d, th * 512:(th + 1) * 512],
                               [xt_tok[th * 4 + 3], ck]) for d in range(DC)])
                sv = consume(bS, "scalar", lambda e, bS=bS: e.activation(
                    smallb[0:1, 0:512], bS.t[0:1, 0:512], AF.Sigmoid))
                bS2 = nextbank()
                mm_group(bS2, [(bS2.t[:, 0:512], onesb[0:1, :],
                                smallb[0:1, 0:512], [sv])])
                consume(bS2, "vector", lambda e, bS2=bS2, th=th: e.tensor_copy(
                    sigB[:, th * 512:(th + 1) * 512], bS2.t[:, 0:512]))
            sig_done = ("dve", P.cnt["dve"])
            smr[0] = penow()
            smr[1] = penow()

            reg_dbg("xn", xn[:, :], 128, DC * W, BF16)
            reg_dbg("h", hbuf[:, :], 128, DC * W, BF16)
            reg_dbg("w_tm", w_tm[:, :], 128, NTT * NE, F32)
            reg_dbg("sigB", sigB[:, :], 128, W, F32)
            if maxphase < 2:
                bar = {"pe": penow(), "dve": ("dve", P.cnt["dve"]), "act": ("act", P.cnt["act"])}
                continue

            gde = ("dve", P.cnt["dve"])
            emit_scan(0)
            emit_projs(1)
            emit_scan(1)

            reg_dbg("big4", big4[:, :], 128, 4 * DC * 512, BF16)
            if sub < 3:
                bar = {"pe": penow(), "dve": ("dve", P.cnt["dve"]), "act": ("act", P.cnt["act"])}
                continue
            for hf in range(2):
                sl = slots[hf]
                load_slot(sl, wo_src[:, :, hf * 512:(hf + 1) * 512])
                atw[("wo", hf)] = sl
            x1_done = None
            for st in range(2):
                for c in range(DC):
                    bW = nextbank()
                    mm_group(bW, [(bW.t[:, 0:512],
                                   atw[("wo", c // 4)].buf[:, d * 512 + (c % 4) * 128: d * 512 + (c % 4) * 128 + 128],
                                   a_v[:, d, st * 512:(st + 1) * 512],
                                   [attn_last[st], atw[("wo", c // 4)].wtok])
                                  for d in range(DC)])
                    x1_done = consume(bW, "vector", lambda e, bW=bW, c=c, st=st:
                        e.tensor_tensor(
                            y_v[:, c, st * 512:(st + 1) * 512],
                            bW.t[:, 0:512],
                            xn_v[:, c, st * 512:(st + 1) * 512], ALU.add))
            for hf in range(2):
                atw[("wo", hf)].rtok = penow()

            reg_dbg("attn", abuf[:, :], 128, DC * W, BF16)
            reg_dbg("x1", ybuf[:, :], 128, DC * W, BF16)
            if maxphase < 3:
                bar = {"pe": penow(), "dve": ("dve", P.cnt["dve"]), "act": ("act", P.cnt["act"])}
                continue

            h2_guard = penow()
            h2_toks = [None, None]
            for th in range(2):
                bL = nextbank()
                mm_group(bL, [(bL.t[0:1, 0:512], onesb[:, 0:1],
                               y_v[:, d, th * 512:(th + 1) * 512], [x1_done])
                              for d in range(DC)])
                bQ = nextbank()
                sq_prev_mm = None
                for dg in range(4):
                    sq_toks = []
                    for dl in range(2):
                        d = dg * 2 + dl
                        sqb = smallb[:, dl * 512:(dl + 1) * 512]
                        sqt = P.do("scalar", lambda e, d=d, th=th, sqb=sqb: e.activation(
                            sqb, y_v[:, d, th * 512:(th + 1) * 512], AF.Square),
                            waits=[x1_done, sq_prev_mm], inc="act")
                        sq_toks.append((sqb, sqt))
                    for dl, (sqb, sqt) in enumerate(sq_toks):
                        ww = [sqt]
                        if dg == 0 and dl == 0:
                            ww.extend(bQ.rtok)

                        def f(e, sqb=sqb, dg=dg, dl=dl, bQ=bQ):
                            return e.matmul(bQ.t[0:1, 0:512], onesb[:, 0:1], sqb,
                                            start=(dg == 0 and dl == 0),
                                            stop=(dg == 3 and dl == 1))
                        tok = P.do("tensor", f, waits=ww,
                                   inc=("pe" if dl == 1 else None))
                    sq_prev_mm = tok
                bQ.wtok = tok
                m_ = scr[0:1, 4096:4608]
                msq = scr[0:1, 4608:5120]
                var = scr[0:1, 5120:5632]
                rstd = scr[0:1, 5120:5632]
                l1 = consume(bL, "vector", lambda e, bL=bL, m_=m_: e.tensor_scalar(
                    m_, bL.t[0:1, 0:512], 1.0 / D, None, ALU.mult))
                l2 = P.do("vector", lambda e, m_=m_, msq=msq: e.tensor_tensor(
                    msq, m_, m_, ALU.mult), waits=[l1], inc="dve")
                l3a = consume(bQ, "vector", lambda e, bQ=bQ, var=var: e.tensor_scalar(
                    var, bQ.t[0:1, 0:512], 1.0 / D, None, ALU.mult),
                    waits=[l2])
                l3 = P.do("vector", lambda e, var=var, msq=msq: e.tensor_tensor(
                    var, var, msq, ALU.subtract), inc="dve")
                l4 = P.do("scalar", lambda e, var=var: e.activation(
                    var, var, AF.Ln, bias=stats[0:1, 63:64]), waits=[l3], inc="act")
                l5 = P.do("scalar", lambda e, var=var, rstd=rstd: e.activation(
                    rstd, var, AF.Exp, scale=-0.5), waits=[l4], inc="act")
                l6 = P.do("vector", lambda e, m_=m_, rstd=rstd, msq=msq: e.tensor_tensor(
                    msq, m_, rstd, ALU.mult), waits=[l5], inc="dve")
                l7 = P.do("scalar", lambda e, rstd=rstd: e.copy(
                    smallb[0:1, 1024:1536], rstd), waits=[l5], inc="act")
                l8 = P.do("scalar", lambda e, msq=msq: e.copy(
                    smallb[0:1, 1536:2048], msq), waits=[l6], inc="act")
                bR1 = nextbank()
                mm_group(bR1, [(bR1.t[:, 0:512], onesb[0:1, :], smallb[0:1, 1024:1536], [l7])])
                bR2 = nextbank()
                mm_group(bR2, [(bR2.t[:, 0:512], onesb[0:1, :], smallb[0:1, 1536:2048], [l8])])
                htk = None
                for d in range(DC):
                    w1_ = P.do("vector", lambda e, bR1=bR1, d=d, th=th: e.tensor_tensor(
                        scr[:, 0:512], y_v[:, d, th * 512:(th + 1) * 512],
                        bR1.t[:, 0:512], ALU.mult),
                        waits=[bR1.wtok, x1_done], inc="dve")
                    w2_ = P.do("vector", lambda e, bR2=bR2: e.tensor_tensor(
                        scr[:, 1024:1536], scr[:, 0:512], bR2.t[:, 0:512], ALU.subtract),
                        waits=[bR2.wtok, w1_], inc="dve")
                    htk = P.do("vector", lambda e, d=d, th=th: e.tensor_scalar(
                        hbuf[:, d * W + th * 512: d * W + (th + 1) * 512],
                        scr[:, 1024:1536],
                        g2c[:, d:d + 1], be2c[:, d:d + 1], ALU.mult, ALU.add),
                        waits=[w2_, h2_guard, cs], inc="dve")
                bR1.rtok = [htk]
                bR2.rtok = [htk]
                h2_toks[th] = htk

            reg_dbg("h2", hbuf[:, :], 128, DC * W, BF16)
            gelu_guard = penow()
            s_done = None
            for st in range(2):
                gel_toks = []
                w1sl = [slots[i] for i in range(3)]
                w2sl = [slots[3 + i] for i in range(3)]
                for fb in range(3):
                    load_slot(w1sl[fb], w1_src[:, :, fb * 512:(fb + 1) * 512])
                for fb in range(3):
                    load_slot(w2sl[fb], w2_src[:, fb * 4:(fb + 1) * 4, :])
                for fc in range(32):
                    fb = fc // 4
                    sl = w1sl[fb % 3]
                    bF = nextbank()
                    mm_group(bF, [(bF.t[:, 0:512],
                                   sl.buf[:, d * 512 + (fc % 4) * 128: d * 512 + (fc % 4) * 128 + 128],
                                   h_v[:, d, st * 512:(st + 1) * 512],
                                   [h2_toks[st], sl.wtok]) for d in range(DC)])
                    if fc % 4 == 3:
                        sl.rtok = bF.wtok
                        if fb + 3 <= 7:
                            load_slot(sl, w1_src[:, :, (fb + 3) * 512:(fb + 4) * 512])
                    gel_toks.append(consume(bF, "scalar", lambda e, bF=bF, fc=fc: e.activation(
                        big4[:, fc * 512:(fc + 1) * 512], bF.t[:, 0:512], AF.Gelu,
                        bias=b1c[:, fc:fc + 1]), waits=[gelu_guard, cs]))
                for fc in range(32):
                    fb = fc // 4
                    sl = w2sl[fb % 3]
                    for dout in range(DC):
                        bk = banks[dout]
                        w = [gel_toks[fc], sl.wtok]
                        if fc == 0:
                            w.extend(bk.rtok)

                        def f(e, bk=bk, sl=sl, fc=fc, dout=dout):
                            return e.matmul(
                                bk.t[:, 0:512],
                                sl.buf[:, (fc % 4) * 1024 + dout * 128:(fc % 4) * 1024 + (dout + 1) * 128],
                                big4[:, fc * 512:(fc + 1) * 512],
                                start=(fc == 0), stop=(fc == 31))
                        tok = P.do("tensor", f, waits=w,
                                   inc=("pe" if dout == DC - 1 else None))
                    if fc == 31:
                        for dout in range(DC):
                            banks[dout].wtok = tok
                    if fc % 4 == 3:
                        sl.rtok = tok
                        if fb + 3 <= 7:
                            load_slot(sl, w2_src[:, (fb + 3) * 4:(fb + 4) * 4, :])
                for dout in range(DC):
                    bk = banks[dout]
                    s_done = consume(bk, "vector", lambda e, bk=bk, dout=dout, st=st:
                        e.scalar_tensor_tensor(
                            a_v[:, dout, st * 512:(st + 1) * 512],
                            bk.t[:, 0:512], b2c[:, dout:dout + 1],
                            y_v[:, dout, st * 512:(st + 1) * 512],
                            ALU.add, ALU.add), waits=[cs])
                gelu_guard = penow()
            reg_dbg("s", abuf[:, :], 128, DC * W, BF16)
            if maxphase < 4:
                bar = {"pe": penow(), "dve": ("dve", P.cnt["dve"]), "act": ("act", P.cnt["act"])}
                continue

            x1_free = s_done
            hall_guard = penow()
            y_done = None
            for batch in range(4):
                es = [batch * 4 + i for i in range(4)]
                gsl, usl, dsl = {}, {}, {}
                for i, e_ in enumerate(es[:3]):
                    gsl[e_] = slots[2 * i]
                    load_slot(slots[2 * i], WeG_src[e_])
                    usl[e_] = slots[2 * i + 1]
                    load_slot(slots[2 * i + 1], WeU_src[e_])
                gsl[es[3]], usl[es[3]] = slots[0], slots[1]
                dsl[es[0]], dsl[es[1]] = slots[2], slots[3]
                dsl[es[2]], dsl[es[3]] = slots[4], slots[5]
                a_done = {}
                for ei, e_ in enumerate(es):
                    wb = {0: banks[6], 1: banks[7]}
                    mm_group(wb[0], [(wb[0].t[:, 0:512],
                                      onehot[0:NE, e_ * 128:(e_ + 1) * 128],
                                      wT[0:NE, 0:512], [wT_tok, cs])])
                    mm_group(wb[1], [(wb[1].t[:, 0:512],
                                      onehot[0:NE, e_ * 128:(e_ + 1) * 128],
                                      wT[0:NE, 512:1024], [wT_tok])])
                    last_h = None
                    prev_t2 = None
                    for th in range(2):
                        for hc in range(4):
                            bG = nextbank()
                            mm_group(bG, [(bG.t[:, 0:512],
                                           gsl[e_].buf[:, d * 512 + hc * 128: d * 512 + hc * 128 + 128],
                                           xn_v[:, d, th * 512:(th + 1) * 512],
                                           [gsl[e_].wtok, xt_tok[th * 4 + 3]]) for d in range(DC)])
                            bU = nextbank()
                            mm_group(bU, [(bU.t[:, 0:512],
                                           usl[e_].buf[:, d * 512 + hc * 128: d * 512 + hc * 128 + 128],
                                           xn_v[:, d, th * 512:(th + 1) * 512],
                                           [usl[e_].wtok]) for d in range(DC)])
                            sg = consume(bG, "scalar", lambda e, bG=bG: e.activation(
                                scr[:, 0:512], bG.t[:, 0:512], AF.Silu), waits=[prev_t2])
                            t2 = consume(bU, "vector", lambda e, bU=bU: e.tensor_tensor(
                                scr[:, 1024:1536], scr[:, 0:512], bU.t[:, 0:512], ALU.mult),
                                waits=[sg])
                            prev_t2 = t2
                            last_h = P.do("vector", lambda e, wbb=wb[th], ei=ei, hc=hc, th=th:
                                e.tensor_tensor(
                                    big4[:, ei * 4096 + hc * 1024 + th * 512:
                                         ei * 4096 + hc * 1024 + (th + 1) * 512],
                                    scr[:, 1024:1536], wbb.t[:, 0:512], ALU.mult),
                                waits=[wb[th].wtok, t2, hall_guard, x1_free], inc="dve")
                            a_done[(ei, th, hc)] = last_h
                    wb[0].rtok = [last_h]
                    wb[1].rtok = [last_h]
                    gsl[e_].rtok = penow()
                    usl[e_].rtok = penow()
                    if ei == 0:
                        load_slot(slots[0], WeG_src[es[3]])
                        load_slot(slots[1], WeU_src[es[3]])
                    elif ei == 1:
                        load_slot(slots[2], WeD_src[es[0]])
                        load_slot(slots[3], WeD_src[es[1]])
                    elif ei == 2:
                        load_slot(slots[4], WeD_src[es[2]])
                        load_slot(slots[5], WeD_src[es[3]])
                for th in range(2):
                    for dout in range(DC):
                        bk = banks[dout]
                        idx = 0
                        tok = None
                        for ei, e_ in enumerate(es):
                            for hc in range(4):
                                w = [dsl[e_].wtok, a_done[(ei, th, hc)]]
                                if idx == 0:
                                    w.extend(bk.rtok)

                                def f(e, bk=bk, sl=dsl[e_], ei=ei, hc=hc, th=th, dout=dout,
                                      idx=idx):
                                    return e.matmul(
                                        bk.t[:, 0:512],
                                        sl.buf[:, hc * 1024 + dout * 128: hc * 1024 + (dout + 1) * 128],
                                        big4[:, ei * 4096 + hc * 1024 + th * 512:
                                             ei * 4096 + hc * 1024 + (th + 1) * 512],
                                        start=(idx == 0), stop=(idx == 15))
                                tok = P.do("tensor", f, waits=w,
                                           inc=("pe" if idx == 15 else None))
                                idx += 1
                        bk.wtok = tok
                        if batch == 0:
                            y_done = consume(bk, "vector",
                                lambda e, bk=bk, dout=dout, th=th: e.tensor_copy(
                                    y_v[:, dout, th * 512:(th + 1) * 512], bk.t[:, 0:512]),
                                waits=[x1_free])
                        else:
                            y_done = consume(bk, "vector",
                                lambda e, bk=bk, dout=dout, th=th: e.tensor_tensor(
                                    y_v[:, dout, th * 512:(th + 1) * 512],
                                    y_v[:, dout, th * 512:(th + 1) * 512],
                                    bk.t[:, 0:512], ALU.add))
                for e_ in es:
                    dsl[e_].rtok = penow()
                hall_guard = penow()
            reg_dbg("ymoe", ybuf[:, :], 128, DC * W, BF16)
            if maxphase < 5:
                bar = {"pe": penow(), "dve": ("dve", P.cnt["dve"]), "act": ("act", P.cnt["act"])}
                continue

            prev_ptr = None
            def emit_p6(tt0, tt1, gtok):
                nonlocal prev_ptr
                for tt in range(tt0, tt1):
                    xr = syncdma(xin[tt % 2],
                                 x_h[seq, row0 + tt * 128: row0 + (tt + 1) * 128, :],
                                 waits=[last_x_use[tt % 2]])
                    pv = P.do("vector", lambda e, tt=tt: e.tensor_tensor(
                        scr[:, 1024:2048].rearrange("q (c w) -> q c w", w=128),
                        a_v[:, :, tt * 128:(tt + 1) * 128],
                        y_v[:, :, tt * 128:(tt + 1) * 128], ALU.mult),
                        waits=[gtok, s_done, prev_ptr], inc="dve")
                    otoks = []
                    for g in range(2):
                        bT = nextbank()
                        tr_group(bT, [(bT.t[:, j * 128:(j + 1) * 128],
                                       scr[:, 1024 + (g * 4 + j) * 128:1024 + (g * 4 + j + 1) * 128],
                                       [pv, cs]) for j in range(4)])
                        prev_ptr = bT.wtok
                        otoks.append(consume(bT, "vector",
                            lambda e, bT=bT, tt=tt, g=g: e.tensor_tensor(
                                scr[:, 4096 + (tt % 2) * 1024 + g * 512:
                                    4096 + (tt % 2) * 1024 + (g + 1) * 512],
                                bT.t[:, 0:512],
                                xin[tt % 2][:, g * 512:(g + 1) * 512], ALU.add),
                            waits=[xr, outst_free[tt % 2]]))
                    stok = storedma(y_h[seq, row0 + tt * 128: row0 + (tt + 1) * 128, :],
                                    scr[:, 4096 + (tt % 2) * 1024: 4096 + (tt % 2 + 1) * 1024],
                                    waits=otoks)
                    outst_free[tt % 2] = stok
                    last_x_use[tt % 2] = otoks[-1]
            hb_guard = penow()
            prev_hbm = None
            hb_dst = [hbuf, big4]
            sG, sU, sD = {}, {}, {}
            for j in range(3):
                sG[j] = slots[2 * j]
                load_slot(slots[2 * j], WsG_src[:, :, j * 512:(j + 1) * 512])
                sU[j] = slots[2 * j + 1]
                load_slot(slots[2 * j + 1], WsU_src[:, :, j * 512:(j + 1) * 512])
            sG[3], sU[3] = slots[0], slots[1]
            sD[0], sD[1], sD[2], sD[3] = slots[2], slots[3], slots[4], slots[5]
            hb_toks = {}
            for j in range(4):
                for hcl in range(4):
                    hc = j * 4 + hcl
                    for th in range(2):
                        bG = nextbank()
                        mm_group(bG, [(bG.t[:, 0:512],
                                       sG[j].buf[:, d * 512 + hcl * 128: d * 512 + hcl * 128 + 128],
                                       xn_v[:, d, th * 512:(th + 1) * 512],
                                       [sG[j].wtok, xt_tok[th * 4 + 3]]) for d in range(DC)])
                        bU = nextbank()
                        mm_group(bU, [(bU.t[:, 0:512],
                                       sU[j].buf[:, d * 512 + hcl * 128: d * 512 + hcl * 128 + 128],
                                       xn_v[:, d, th * 512:(th + 1) * 512],
                                       [sU[j].wtok]) for d in range(DC)])
                        sg = consume(bG, "scalar", lambda e, bG=bG: e.activation(
                            scr[:, 0:512], bG.t[:, 0:512], AF.Silu), waits=[prev_hbm])
                        hb_toks[(th, hc)] = consume(bU, "vector",
                            lambda e, bU=bU, hc=hc, th=th: e.tensor_tensor(
                                hb_dst[th][:, hc * 512:(hc + 1) * 512],
                                scr[:, 0:512], bU.t[:, 0:512], ALU.mult),
                            waits=[sg, hb_guard])
                        prev_hbm = hb_toks[(th, hc)]
                sG[j].rtok = penow()
                sU[j].rtok = penow()
                if j == 0:
                    load_slot(slots[0], WsG_src[:, :, 3 * 512:4 * 512])
                    load_slot(slots[1], WsU_src[:, :, 3 * 512:4 * 512])
                elif j == 1:
                    load_slot(slots[2], WsD_src[:, 0:4, :])
                    load_slot(slots[3], WsD_src[:, 4:8, :])
                elif j == 2:
                    load_slot(slots[4], WsD_src[:, 8:12, :])
                    load_slot(slots[5], WsD_src[:, 12:16, :])
            g_done = None
            for th in range(2):
                for dout in range(DC):
                    bk = banks[dout]
                    idx = 0
                    tok = None
                    for hc in range(16):
                        w = [sD[hc // 4].wtok, hb_toks[(th, hc)]]
                        if idx == 0:
                            w.extend(bk.rtok)

                        def f(e, bk=bk, sl=sD[hc // 4], hc=hc, th=th, dout=dout, idx=idx):
                            return e.matmul(
                                bk.t[:, 0:512],
                                sl.buf[:, (hc % 4) * 1024 + dout * 128:(hc % 4) * 1024 + (dout + 1) * 128],
                                hb_dst[th][:, hc * 512:(hc + 1) * 512],
                                start=(idx == 0), stop=(idx == 15))
                        tok = P.do("tensor", f, waits=w,
                                   inc=("pe" if idx == 15 else None))
                        idx += 1
                    bk.wtok = tok
                    v1 = consume(bk, "vector", lambda e, bk=bk, th=th: e.tensor_tensor(
                        scr[:, 0:512], bk.t[:, 0:512],
                        sigB[:, th * 512:(th + 1) * 512], ALU.mult),
                        waits=[sig_done])
                    v2 = P.do("vector", lambda e, dout=dout, th=th: e.tensor_tensor(
                        scr[:, 512:1024], scr[:, 0:512],
                        y_v[:, dout, th * 512:(th + 1) * 512], ALU.add),
                        waits=[v1, y_done, g_done], inc="dve")
                    g_done = P.do("scalar", lambda e, dout=dout, th=th: e.activation(
                        y_v[:, dout, th * 512:(th + 1) * 512],
                        scr[:, 512:1024], AF.Silu), waits=[v2], inc="act")
                emit_p6(th * 4, th * 4 + 4, g_done)
            for j in range(4):
                sD[j].rtok = penow()
            act_p5_end = P.cnt["act"]
            reg_dbg("g", ybuf[:, :], 128, DC * W, BF16)
            if maxphase < 6:
                bar = {"pe": penow(), "dve": ("dve", P.cnt["dve"]), "act": ("act", P.cnt["act"])}
                continue

            bar = {"pe": penow(), "dve": ("dve", P.cnt["dve"]), "act": ("act", P.cnt["act"])}

        assert wload["i"] <= 128, wload["i"]
        waitall = [("pe", P.cnt["pe"]), ("dve", P.cnt["dve"]), ("act", P.cnt["act"]),
                   ("wl", P.cnt["wl"]), ("xl", P.cnt["xl"])]
        for name, (ap, out) in dbg_regs.items():
            P.do("gpsimd", lambda e, ap=ap, out=out: e.dma_start(out, ap),
                 waits=waitall, inc="so", amt=16)
        P.do("gpsimd", lambda e: e.nop(),
             waits=[("so", P.cnt["so"])] + waitall)
        P.finalize(stack)
    return nc


def _host_pack(inputs):
    f = np.float32
    asc = np.ascontiguousarray
    d = {}
    d["wq"] = asc(inputs["L0_attn_wq"], dtype=f)
    d["wk"] = asc(inputs["L0_attn_wk"], dtype=f)
    d["wv"] = asc(inputs["L0_attn_wv"], dtype=f)
    d["wo"] = asc(inputs["L0_attn_wo"], dtype=f)
    pj = np.zeros((128, 128), dtype=f)
    pj[0:64, 0:64] = np.asarray(inputs["L0_attn_proj"], dtype=f)
    pj[64:128, 64:128] = pj[0:64, 0:64]
    d["proj"] = pj
    d["w1"] = asc(inputs["L0_attn_w1"], dtype=f)
    d["w2"] = asc(inputs["L0_attn_w2"], dtype=f)
    d["b1c"] = asc(np.asarray(inputs["L0_attn_b1"], dtype=f).reshape(FF // 128, 128).T)
    d["b2c"] = asc(np.asarray(inputs["L0_attn_b2"], dtype=f).reshape(DC, 128).T)
    d["g1c"] = asc(np.asarray(inputs["L0_attn_g1"], dtype=f).reshape(DC, 128).T)
    d["be1c"] = asc(np.asarray(inputs["L0_attn_beta1"], dtype=f).reshape(DC, 128).T)
    d["g2c"] = asc(np.asarray(inputs["L0_attn_g2"], dtype=f).reshape(DC, 128).T)
    d["be2c"] = asc(np.asarray(inputs["L0_attn_beta2"], dtype=f).reshape(DC, 128).T)
    d["WgT"] = asc(np.asarray(inputs["L0_ffn_Wg"], dtype=f).T)
    d["WeGT"] = asc(np.asarray(inputs["L0_ffn_We_gate"], dtype=f).transpose(0, 2, 1))
    d["WeUT"] = asc(np.asarray(inputs["L0_ffn_We_up"], dtype=f).transpose(0, 2, 1))
    d["WeDT"] = asc(np.asarray(inputs["L0_ffn_We_down"], dtype=f).transpose(0, 2, 1))
    d["WsGT"] = asc(np.asarray(inputs["L0_ffn_Ws_gate"], dtype=f).T)
    d["WsUT"] = asc(np.asarray(inputs["L0_ffn_Ws_up"], dtype=f).T)
    d["WsDT"] = asc(np.asarray(inputs["L0_ffn_Ws_down"], dtype=f).T)
    d["WsgT"] = asc(np.asarray(inputs["L0_ffn_Wsg"], dtype=f).reshape(DC, 128).T)
    d["idf"] = np.eye(128, dtype=f)
    d["maskT"] = asc(np.tile(np.triu(np.ones((128, 128), dtype=f)), (1, 4)))
    d["onehot"] = asc(np.kron(np.eye(NE), np.ones((1, 128))).astype(ml_dtypes.bfloat16))
    d["onesf"] = np.ones((128, 128), dtype=f)
    d["sel8"] = asc(np.kron(np.eye(8, dtype=f), np.ones((1, 64), dtype=f)))
    d["onesb"] = np.ones((128, 128), dtype=ml_dtypes.bfloat16)
    return d


def kernel(**inputs):
    x = np.ascontiguousarray(np.asarray(inputs["L0_attn_x"], dtype=np.float32))
    shared = _host_pack(inputs)
    nc = bass.Bass("TRN2", target_bir_lowering=False)
    build(nc)
    in_maps = []
    for c in range(NCORE):
        m = dict(shared)
        m["x"] = np.ascontiguousarray(x[c * BPC:(c + 1) * BPC])
        in_maps.append(m)
    res = run_bass_kernel_spmd(nc, in_maps, core_ids=list(range(NCORE)))
    return np.concatenate([r["y"] for r in res.results], axis=0)
